# version 07
# baseline (speedup 1.0000x reference)
import numpy as np
import ml_dtypes
from contextlib import ExitStack

import concourse.bass as bass
import concourse.mybir as mybir
from concourse.bass_utils import run_bass_kernel_spmd

F32 = mybir.dt.float32
F32R = mybir.dt.float32r
BF16 = mybir.dt.bfloat16
AF = mybir.ActivationFunctionType
ALU = mybir.AluOpType
AX = mybir.AxisListType.X

NCORE = 8
NCHUNK = 8
T = 512
KC = 16
NH = 16
NE = 8
D = 2048
GC = 0.7978845608028654
C3 = 0.044715

OFF_APAR, OFF_IGB, OFF_AGB, OFF_RS, OFF_RL, OFF_PES, OFF_ID128, OFF_ID8, OFF_RSTA, OFF_RSTB = (
    0, 16, 32, 48, 64, 192, 200, 328, 336, 344)
OFF_FC = 352
FCONSTS = (1.0, 1e-5, 1e-6, -0.5 * float(np.log(2048.0)), -1.0, 0.5, -0.5,
           0.044715, 0.7978845608028654, 2048.0)
S32 = OFF_FC + len(FCONSTS)
SR = 128


def build_program():
    nc = bass.Bass("TRN2", target_bir_lowering=False, detect_race_conditions=False)
    dt_f = nc.dram_tensor

    xin = dt_f("xin", [NCHUNK, 128, KC, T], F32R, kind="ExternalInput").ap()
    gtw = dt_f("gtw", [NE, 2, 8, 128, KC, 256], F32, kind="ExternalInput").ap()
    ltw = dt_f("ltw", [NE, KC, 128, KC, 128], F32, kind="ExternalInput").ap()
    igw_d = dt_f("igw_d", [NH, 128, 128], F32R, kind="ExternalInput").ap()
    agw_d = dt_f("agw_d", [NH, 128, 128], F32R, kind="ExternalInput").ap()
    sm32_d = dt_f("sm32_d", [128, S32], F32, kind="ExternalInput").ap()
    smr_d = dt_f("smr_d", [128, SR], F32R, kind="ExternalInput").ap()
    yout = dt_f("yout", [2, 128, KC, T], F32, kind="ExternalOutput").ap()

    KB = 1024
    BASE = 16512
    A = lambda name, shape, dtype, offset: nc.alloc_sbuf_tensor_at(
        name, shape, dtype, offset=BASE + offset)
    x1f0  = A("x1f0",  [128, KC, T], F32R, offset=0)
    ybuf  = A("ybuf",  [128, KC, T], F32,  offset=32 * KB)
    x1f1  = A("x1f1",  [128, KC, T], F32R, offset=32 * KB)
    x1b0  = A("x1b0",  [128, KC, T], BF16, offset=64 * KB)
    x1b1  = A("x1b1",  [128, KC, T], BF16, offset=80 * KB)
    sqbA  = A("sqbA",  [128, 8, T], BF16,  offset=80 * KB)
    lsb   = A("lsb",   [8, T], F32R,       offset=88 * KB)
    rowA  = A("rowA",  [1, T], F32R,       offset=90 * KB)
    rowB  = A("rowB",  [1, T], F32,        offset=92 * KB)
    rowC  = A("rowC",  [1, T], F32R,       offset=94 * KB)
    xfeat = A("xfeat", [128, KC, T], F32R, offset=96 * KB)
    actb0 = A("actb0", [128, KC, T], BF16, offset=96 * KB)
    actb1 = A("actb1", [128, KC, T], BF16, offset=112 * KB)
    sqbB  = A("sqbB",  [128, 8, T], BF16,  offset=96 * KB)
    rowA2 = A("rowA2", [1, T], F32R,       offset=104 * KB)
    rowB2 = A("rowB2", [1, T], F32,        offset=106 * KB)
    rowC2 = A("rowC2", [1, T], F32R,       offset=108 * KB)
    abuf  = A("abuf",  [128, KC, T], F32,  offset=128 * KB)
    gateT = A("gateT", [128, 3, KC, 256], BF16, offset=128 * KB)
    linbb = A("linbb", [128, 4, KC, 128], BF16, offset=152 * KB)
    xtb   = A("xtb",   [128, KC, T], F32,  offset=160 * KB)
    gelu0 = A("gelu0", [128, 4, T], BF16,  offset=168 * KB)
    T3    = A("T3",    [128, 2, T], F32,   offset=172 * KB)
    T4    = A("T4",    [128, 2, T], F32,   offset=176 * KB)
    cwb0  = A("cwb0",  [128, T], F32,      offset=180 * KB)
    cwb1  = A("cwb1",  [128, T], F32,      offset=182 * KB)
    rowE  = A("rowE",  [1, T], F32R,       offset=184 * KB)
    cwT1  = A("cwT1",  [8, T], F32R,       offset=186 * KB)
    PRM = 192 * KB
    igwin = A("igwin", [128, 4, 128], F32R, offset=PRM)
    agwin = A("agwin", [128, 4, 128], F32R, offset=PRM + 2 * KB)
    sm32  = A("sm32",  [128, S32], F32, offset=PRM + 4 * KB)
    smr   = A("smr",   [128, SR], F32R, offset=PRM + 4 * KB + 1472)
    cwT0  = A("cwT0",  [8, T], F32R,    offset=PRM + 6 * KB)
    nsp   = A("nsp",   [128, NH], F32,  offset=PRM + 8 * KB)
    spt   = A("spt",   [128, NH], F32,  offset=PRM + 8 * KB + 64)
    carry = A("carry", [128, KC], F32,  offset=PRM + 8 * KB + 128)
    ones_bf = A("ones_bf", [128, 1], BF16, offset=PRM + 8 * KB + 192)
    rlp   = A("rlp",   [128, KC, 8], F32R, offset=PRM + 8 * KB + 224)
    TIN = PRM + 8 * KB + 736
    cwsb  = A("cwsb",  [128, 8], F32,  offset=TIN)
    m1sb  = A("m1sb",  [128, 8], F32,  offset=TIN + 32)
    m2sb  = A("m2sb",  [128, 8], F32,  offset=TIN + 64)
    exsb  = A("exsb",  [128, 8], F32,  offset=TIN + 96)
    l2sb  = A("l2sb",  [128, 8], F32,  offset=TIN + 128)
    colA  = A("colA",  [128, 1], F32,  offset=TIN + 160)
    colB  = A("colB",  [128, 1], F32,  offset=TIN + 192)

    apar_c = sm32.ap()[:, OFF_APAR:OFF_APAR + 16]
    igb_c = sm32.ap()[:, OFF_IGB:OFF_IGB + 16]
    agb_c = sm32.ap()[:, OFF_AGB:OFF_AGB + 16]
    rs_c = sm32.ap()[:, OFF_RS:OFF_RS + 16]
    rl_c = sm32.ap()[:, OFF_RL:OFF_RL + 128]
    pes_c = sm32.ap()[:, OFF_PES:OFF_PES + 8]
    id128 = sm32.ap()[:, OFF_ID128:OFF_ID128 + 128]
    id8 = sm32.ap()[0:8, OFF_ID8:OFF_ID8 + 8]
    rsta_c = sm32.ap()[:, OFF_RSTA:OFF_RSTA + 8]
    rstb_c = sm32.ap()[:, OFF_RSTB:OFF_RSTB + 8]
    for i, val in enumerate(FCONSTS):
        nc.const_aps.register(F32, val, sm32.ap()[:, OFF_FC + i:OFF_FC + i + 1])
    onescol_r = smr.ap()[:, 0:1]
    onesrow_r = smr.ap()[0:1, 0:128]
    ones8row_r = smr.ap()[0:1, 0:8]

    es = ExitStack()
    P = [es.enter_context(nc.psum_tensor(f"P{i}", [128, T], F32)) for i in range(8)]
    SEMS = {}
    for s in ("SX", "SW", "SP", "SA", "SV", "SL", "SO"):
        SEMS[s] = es.enter_context(nc.semaphore(s))
    block = es.enter_context(nc.Block())

    DMA_SEMS = {"SX", "SW", "SO"}
    streams = {e: [] for e in ("sync", "tensor", "scalar", "vector", "gpsimd")}
    cnt = {s: 0 for s in SEMS}

    def ev(eng, fn, waits=None, inc=None):
        streams[eng].append((dict(waits or {}), fn, inc))
        if inc:
            cnt[inc] += 16 if inc in DMA_SEMS else 1
            return cnt[inc]
        return None

    def merge(*wds):
        out = {}
        for wd in wds:
            if not wd:
                continue
            for k, v in wd.items():
                out[k] = max(out.get(k, 0), v)
        return out

    w_sm = {}
    v = ev("sync", lambda e: e.dma_start(sm32.ap()[:, :], sm32_d), inc="SX")
    v = ev("sync", lambda e: e.dma_start(smr.ap()[:, :], smr_d), inc="SX")
    w_sm["SX"] = v

    ev("scalar", lambda e: e.activation(spt.ap()[:, :], apar_c, AF.Abs), waits=w_sm)
    ev("scalar", lambda e: e.activation(spt.ap()[:, :], spt.ap()[:, :], AF.Exp, scale=-1.0))
    v = ev("scalar", lambda e: e.activation(spt.ap()[:, :], spt.ap()[:, :], AF.Ln, bias=1.0), inc="SA")
    v = ev("vector", lambda e: e.tensor_scalar_max(nsp.ap()[:, :], apar_c, 0.0),
           waits=merge({"SA": v}, w_sm), inc="SV")
    v = ev("vector", lambda e: e.tensor_add(nsp.ap()[:, :], nsp.ap()[:, :], spt.ap()[:, :]),
           waits={"SV": v}, inc="SV")
    v = ev("vector", lambda e: e.tensor_scalar_mul(nsp.ap()[:, :], nsp.ap()[:, :], -8.0),
           waits={"SV": v}, inc="SV")
    for k in range(KC):
        def frl(e, k=k):
            return e.tensor_scalar_mul(rlp.ap()[:, k, :], rl_c[:, k * 8:(k + 1) * 8], rs_c[:, k:k + 1])
        v = ev("vector", frl, inc="SV")
    v = ev("vector", lambda e: e.memset(carry.ap()[:, :], 0.0), inc="SV")
    v = ev("vector", lambda e: e.memset(ones_bf.ap()[:, :], 1.0), inc="SV")
    w_setup_v = {"SV": cnt["SV"], "SA": cnt["SA"]}

    bankfree = [dict() for _ in range(8)]
    xfeat_free, abuf_free, xtb_free, ybuf_free = {}, {}, {}, {}
    gateT_free = [dict(), dict(), dict()]
    linb_free = [dict(), dict(), dict(), dict()]
    T3_free = [dict(), dict()]
    T4_free = [dict(), dict()]
    gelu_free = [dict() for _ in range(4)]
    actb_free = [dict(), dict()]
    igwin_free = [dict() for _ in range(4)]
    agwin_free = [dict() for _ in range(4)]
    state = {"carry_v": cnt["SV"], "gm1": 0, "wtile": 0, "glob_e": 0,
             "rowE_free": dict(), "last_igw_mm": 0, "lin_last": [0, 0],
             "scan_sv": [cnt["SV"]] * KC}

    def attn_chunk(cidx, half):
        w_x = {}
        v = ev("sync", lambda e, c=cidx: e.dma_start(xfeat.ap()[:, :, :], xin[c]),
               waits=merge(xfeat_free), inc="SX")
        w_x["SX"] = v
        ig_w = [None] * KC
        ag_w = [None] * KC
        def emit_win_load(hc):
            sl = state["wtile"] % 4
            state["wtile"] += 1
            def fli(e, hc=hc, sl=sl):
                return e.dma_start(igwin.ap()[:, sl, :], igw_d[hc])
            v = ev("gpsimd", fli, waits=igwin_free[sl], inc="SW")
            ig_w[hc] = ({"SW": v}, sl)
            def fla(e, hc=hc, sl=sl):
                return e.dma_start(agwin.ap()[:, sl, :], agw_d[hc])
            v = ev("gpsimd", fla, waits=agwin_free[sl], inc="SW")
            ag_w[hc] = ({"SW": v}, sl)
        for hc in range(4):
            emit_win_load(hc)
        ig_done = [0] * KC
        ag_done = [0] * KC
        sig_done = [0] * KC
        scan_done = [0] * KC
        for hc in range(KC):
            pig, pag = P[2 * (hc % 2)], P[1 + 2 * (hc % 2)]
            wb = merge(w_x, bankfree[2 * (hc % 2)], bankfree[1 + 2 * (hc % 2)], w_setup_v)
            wi, sli = ig_w[hc]
            wa, sla = ag_w[hc]
            def fig(e, pig=pig, hc=hc, sli=sli):
                return e.matmul(pig.ap()[:, :], igwin.ap()[:, sli, :], xfeat.ap()[:, hc, :],
                                start=True, stop=True)
            ig_done[hc] = ev("tensor", fig, waits=merge(wb, wi), inc="SP")
            igwin_free[sli] = {"SP": ig_done[hc]}
            def fag(e, pag=pag, hc=hc, sla=sla):
                return e.matmul(pag.ap()[:, :], agwin.ap()[:, sla, :], xfeat.ap()[:, hc, :],
                                start=True, stop=True)
            ag_done[hc] = ev("tensor", fag, waits=wa, inc="SP")
            agwin_free[sla] = {"SP": ag_done[hc]}
            if hc + 4 < KC:
                emit_win_load(hc + 4)
            def fsx(e, pig=pig, hc=hc):
                return e.activation(xtb.ap()[:, hc, :], pig.ap()[:, :], AF.Sigmoid,
                                    bias=igb_c[:, hc:hc + 1])
            ev("scalar", fsx, waits=merge({"SP": ig_done[hc]},
                                          {"SV": state["scan_sv"][hc]}))
            def fsa(e, pag=pag, hc=hc):
                return e.activation(abuf.ap()[:, hc, :], pag.ap()[:, :], AF.Sigmoid,
                                    bias=agb_c[:, hc:hc + 1])
            sig_done[hc] = ev("scalar", fsa, waits={"SP": ag_done[hc]}, inc="SA")
            bankfree[2 * (hc % 2)] = {"SA": sig_done[hc]}
            bankfree[1 + 2 * (hc % 2)] = {"SA": sig_done[hc]}
            def fexp(e, hc=hc):
                return e.activation(abuf.ap()[:, hc, :], abuf.ap()[:, hc, :], AF.Exp,
                                    scale=nsp.ap()[:, hc:hc + 1])
            ve = ev("scalar", fexp, inc="SA")
            def fa2(e, hc=hc):
                return e.tensor_mul(ybuf.ap()[:, hc, :], abuf.ap()[:, hc, :], abuf.ap()[:, hc, :])
            va = ev("vector", fa2, waits={"SA": ve}, inc="SV")
            def fsq(e, hc=hc):
                return e.activation(ybuf.ap()[:, hc, :], ybuf.ap()[:, hc, :], AF.Sqrt,
                                    scale=-1.0, bias=1.0)
            vq = ev("scalar", fsq, waits={"SV": va}, inc="SA")
            def frst(e, hc=hc, c=cidx):
                return e.tensor_scalar(ybuf.ap()[:, hc, 0:1], ybuf.ap()[:, hc, 0:1],
                                       rsta_c[:, c:c + 1], rstb_c[:, c:c + 1],
                                       op0=ALU.mult, op1=ALU.add)
            vr = ev("vector", frst, waits={"SA": vq}, inc="SV")
            def fxt1(e, hc=hc):
                return e.tensor_mul(xtb.ap()[:, hc, :], xtb.ap()[:, hc, :], xfeat.ap()[:, hc, :])
            vg = ev("gpsimd", fxt1, waits=merge({"SA": sig_done[hc]},
                                                w_x if hc == 0 else None), inc="SL")
            xt1_last = vg
            def fxt2(e, hc=hc):
                return e.tensor_mul(xtb.ap()[:, hc, :], xtb.ap()[:, hc, :], ybuf.ap()[:, hc, :])
            v = ev("vector", fxt2, waits={"SL": vg, "SV": vr}, inc="SV")
            def fsc(e, hc=hc):
                return e.tensor_tensor_scan(ybuf.ap()[:, hc, :], abuf.ap()[:, hc, :],
                                            xtb.ap()[:, hc, :], carry.ap()[:, hc:hc + 1],
                                            op0=ALU.mult, op1=ALU.add)
            scan_done[hc] = ev("vector", fsc, inc="SV")
            state["scan_sv"][hc] = scan_done[hc]
        state["last_igw_mm"] = ag_done[KC - 1]
        yf = ybuf.ap().rearrange("p k t -> p (k t)")
        xff = xfeat.ap().rearrange("p k t -> p (k t)")
        def fcu(e):
            return e.tensor_copy(carry.ap()[:, :], ybuf.ap()[:, :, T - 1:T])
        v = ev("vector", fcu, waits={"SV": scan_done[KC - 1]}, inc="SV")
        state["carry_v"] = v
        if half is not None:
            dst = x1f0 if half == 0 else x1f1
            def fr(e, dst=dst):
                return e.tensor_add(dst.ap().rearrange("p k t -> p (k t)"), xff, yf)
            v = ev("vector", fr, inc="SV")
            xfeat_free.clear()
            xfeat_free["SV"] = v
        else:
            xfeat_free.clear()
            xfeat_free["SL"] = xt1_last
            xfeat_free["SP"] = ag_done[KC - 1]
        abuf_free.clear()
        abuf_free["SV"] = v
        xtb_free.clear()
        xtb_free["SV"] = v
        ybuf_free.clear()
        ybuf_free["SV"] = v
        return v

    def layernorm(x1, eps, wpre, scr, bs0, bs1, bb0, bb1):
        sqb, rA, rB, rC = scr
        sq_done = [[0] * 8, [0] * 8]
        pe_wave = [0, 0]
        for wv in range(2):
            for i in range(8):
                hc = wv * 8 + i
                wq = wpre if (wv == 0 and i == 0) else None
                if wv == 1:
                    wq = merge(wq, {"SP": pe_wave[0]})
                def fq(e, hc=hc, i=i):
                    return e.activation(sqb.ap()[:, i, :], x1.ap()[:, hc, :], AF.Square)
                sq_done[wv][i] = ev("scalar", fq, waits=wq, inc="SA")
            for i in range(8):
                hc = wv * 8 + i
                w0 = merge(bankfree[bs1], {"SA": sq_done[wv][i]}, wpre) if (wv == 0 and i == 0) \
                    else {"SA": sq_done[wv][i]}
                def fs2(e, i=i, hc=hc):
                    return e.matmul(P[bs1].ap()[0:1, :], ones_bf.ap()[:, :], sqb.ap()[:, i, :],
                                    start=(hc == 0), stop=(hc == KC - 1))
                pe_wave[wv] = ev("tensor", fs2, waits=w0, inc="SP")
        sq_sum_done = pe_wave[1]
        for hc in range(KC):
            def fs(e, hc=hc):
                return e.matmul(P[bs0].ap()[0:1, :], onescol_r, x1.ap()[:, hc, :],
                                start=(hc == 0), stop=(hc == KC - 1))
            v = ev("tensor", fs, waits=merge(bankfree[bs0], wpre, w_sm) if hc == 0 else None,
                   inc="SP" if hc == KC - 1 else None)
        sum_done = v
        v = ev("vector", lambda e: e.tensor_scalar_mul(rA.ap()[:, :], P[bs0].ap()[0:1, :], 1.0 / D),
               waits=merge({"SP": sum_done}, wpre), inc="SV")
        v = ev("vector", lambda e: e.scalar_tensor_tensor(rB.ap()[:, :], rA.ap()[:, :], -1.0,
                                                          rA.ap()[:, :], op0=ALU.mult, op1=ALU.mult),
               waits=merge({"SV": v}, wpre), inc="SV")
        v = ev("vector", lambda e: e.scalar_tensor_tensor(rB.ap()[:, :], P[bs1].ap()[0:1, :], 1.0 / D,
                                                          rB.ap()[:, :], op0=ALU.mult, op1=ALU.add),
               waits={"SP": sq_sum_done, "SV": v}, inc="SV")
        bankfree[bs0] = {"SV": v}
        bankfree[bs1] = {"SV": v}
        ev("scalar", lambda e: e.activation(rB.ap()[:, :], rB.ap()[:, :], AF.Ln, bias=eps),
           waits={"SV": v})
        va = ev("scalar", lambda e: e.activation(rC.ap()[:, :], rB.ap()[:, :], AF.Exp, scale=-0.5),
                inc="SA")
        v2 = ev("vector", lambda e: e.scalar_tensor_tensor(rA.ap()[:, :], rA.ap()[:, :], -1.0,
                                                           rC.ap()[:, :], op0=ALU.mult, op1=ALU.mult),
                waits={"SA": va}, inc="SV")
        vb1 = ev("tensor", lambda e: e.matmul(P[bb0].ap()[:, :], onesrow_r, rC.ap()[:, :],
                                              start=True, stop=True),
                 waits=merge(bankfree[bb0], {"SA": va}), inc="SP")
        vb2 = ev("tensor", lambda e: e.matmul(P[bb1].ap()[:, :], onesrow_r, rA.ap()[:, :],
                                              start=True, stop=True),
                 waits=merge(bankfree[bb1], {"SV": v2}), inc="SP")
        for hc in range(KC):
            def fa1(e, hc=hc):
                return e.tensor_mul(x1.ap()[:, hc, :], x1.ap()[:, hc, :], P[bb0].ap()[:, :])
            ev("vector", fa1, waits={"SP": max(vb1, vb2)} if hc == 0 else None)
            def fa2(e, hc=hc):
                return e.tensor_add(x1.ap()[:, hc, :], x1.ap()[:, hc, :], P[bb1].ap()[:, :])
            v = ev("vector", fa2, inc="SV")
        bankfree[bb0] = {"SV": v}
        bankfree[bb1] = {"SV": v}
        return v

    def router(x1, cwT, ln_done, w_extra):
        sqr_done = [[0] * 8, [0] * 8]
        pe_wave = [0, 0]
        for wv in range(2):
            for i in range(8):
                hc = wv * 8 + i
                wq = {"SV": ln_done} if (wv == 0 and i == 0) else None
                if wv == 1:
                    wq = merge(wq, {"SP": pe_wave[0]})
                def fq(e, hc=hc, i=i):
                    return e.activation(sqbA.ap()[:, i, :], x1.ap()[:, hc, :], AF.Square)
                sqr_done[wv][i] = ev("scalar", fq, waits=wq, inc="SA")
            for i in range(8):
                hc = wv * 8 + i
                w0 = merge(bankfree[5], {"SA": sqr_done[wv][i]}) if (wv == 0 and i == 0) \
                    else {"SA": sqr_done[wv][i]}
                def fs2(e, i=i, hc=hc):
                    return e.matmul(P[5].ap()[0:1, :], ones_bf.ap()[:, :], sqbA.ap()[:, i, :],
                                    start=(hc == 0), stop=(hc == KC - 1))
                pe_wave[wv] = ev("tensor", fs2, waits=w0, inc="SP")
        v = ev("vector", lambda e: e.tensor_scalar_mul(rowB.ap()[:, :], P[5].ap()[0:1, :], 1.0 / D),
               waits={"SP": pe_wave[1]}, inc="SV")
        bankfree[5] = {"SV": v}
        ev("scalar", lambda e: e.activation(rowB.ap()[:, :], rowB.ap()[:, :], AF.Ln, bias=1e-6),
           waits={"SV": v})
        sc_done = ev("scalar", lambda e: e.activation(rowC.ap()[:, :], rowB.ap()[:, :], AF.Exp,
                                                      scale=-0.5, bias=-0.5 * float(np.log(D))),
                     inc="SA")
        for hc in range(KC):
            def fl(e, hc=hc):
                return e.matmul(P[4].ap()[0:8, :], rlp.ap()[:, hc, :], x1.ap()[:, hc, :],
                                start=(hc == 0), stop=(hc == KC - 1))
            v = ev("tensor", fl, waits=merge(bankfree[4], {"SV": ln_done}) if hc == 0 else None,
                   inc="SP" if hc == KC - 1 else None)
        logits_done = v
        scb_done = ev("tensor", lambda e: e.matmul(P[5].ap()[0:8, :], ones8row_r, rowC.ap()[:, :],
                                                   start=True, stop=True),
                      waits=merge(bankfree[5], {"SA": sc_done}), inc="SP")
        v = ev("vector", lambda e: e.tensor_copy(lsb.ap()[:, :], P[4].ap()[0:8, :]),
               waits={"SP": logits_done}, inc="SV")
        bankfree[4] = {"SV": v}
        v = ev("vector", lambda e: e.tensor_mul(lsb.ap()[:, :], lsb.ap()[:, :], P[5].ap()[0:8, :]),
               waits={"SP": scb_done}, inc="SV")
        bankfree[5] = {"SV": v}
        lsb_done = v
        for tc in range(4):
            def ftr(e, tc=tc):
                return e.transpose(P[6].ap()[:, 0:8], lsb.ap().bitcast(F32)[0:8, tc * 128:(tc + 1) * 128], id8)
            v = ev("tensor", ftr, waits=merge(bankfree[6], {"SV": lsb_done}, w_sm), inc="SP")
            v = ev("vector", lambda e: e.tensor_copy(cwsb.ap()[:, :], P[6].ap()[:, 0:8]),
                   waits={"SP": v}, inc="SV")
            bankfree[6] = {"SV": v}
            v = ev("vector", lambda e: e.reduce_max(colA.ap()[:, :], cwsb.ap()[:, :], axis=AX,
                                                    negate=True), waits={"SV": v}, inc="SV")
            v = ev("vector", lambda e: e.tensor_scalar(m1sb.ap()[:, :], cwsb.ap()[:, :],
                                                       colA.ap()[:, 0:1], 0.0,
                                                       op0=ALU.add, op1=ALU.is_ge),
                   waits={"SV": v}, inc="SV")
            vex = ev("scalar", lambda e: e.activation(exsb.ap()[:, :], cwsb.ap()[:, :], AF.Exp,
                                                      bias=colA.ap()[:, 0:1]),
                     waits={"SV": v}, inc="SA")
            v = ev("vector", lambda e: e.scalar_tensor_tensor(l2sb.ap()[:, :], m1sb.ap()[:, :], -1e30,
                                                              cwsb.ap()[:, :], op0=ALU.mult,
                                                              op1=ALU.add), waits={"SV": v}, inc="SV")
            v = ev("vector", lambda e: e.reduce_max(colB.ap()[:, :], l2sb.ap()[:, :], axis=AX,
                                                    negate=True), waits={"SV": v}, inc="SV")
            v = ev("vector", lambda e: e.tensor_scalar(m2sb.ap()[:, :], l2sb.ap()[:, :],
                                                       colB.ap()[:, 0:1], 0.0,
                                                       op0=ALU.add, op1=ALU.is_ge),
                   waits={"SV": v}, inc="SV")
            v = ev("vector", lambda e: e.tensor_add(m1sb.ap()[:, :], m1sb.ap()[:, :], m2sb.ap()[:, :]),
                   waits={"SV": v}, inc="SV")
            v = ev("vector", lambda e: e.tensor_mul(m1sb.ap()[:, :], m1sb.ap()[:, :], exsb.ap()[:, :]),
                   waits={"SA": vex, "SV": v}, inc="SV")
            v = ev("vector", lambda e: e.reduce_sum(colB.ap()[:, :], m1sb.ap()[:, :], axis=AX),
                   waits={"SV": v}, inc="SV")
            v = ev("vector", lambda e: e.reciprocal(colA.ap()[:, :], colB.ap()[:, :]),
                   waits={"SV": v}, inc="SV")
            v = ev("vector", lambda e: e.tensor_scalar_mul(cwsb.ap()[:, :], m1sb.ap()[:, :],
                                                           colA.ap()[:, 0:1]),
                   waits={"SV": v}, inc="SV")
            def ftr2(e, tc=tc):
                return e.transpose(P[7].ap()[0:8, 0:128], cwsb.ap()[:, 0:8], id128)
            v = ev("tensor", ftr2, waits=merge(bankfree[7], {"SV": v}), inc="SP")
            def fcp(e, tc=tc):
                return e.tensor_copy(cwT.ap()[:, tc * 128:(tc + 1) * 128], P[7].ap()[0:8, 0:128])
            v = ev("vector", fcp, waits=merge({"SP": v}, w_extra if tc == 0 else None), inc="SV")
            bankfree[7] = {"SV": v}
        return cnt["SV"]

    x1f_h = [x1f0, x1f1]
    x1b_h = [x1b0, x1b1]
    actb_h = [actb0, actb1]
    cwb_h = [cwb0, cwb1]
    cwT_h = [cwT0, cwT1]
    cw_done = [dict(), dict()]
    x1b_done = [dict(), dict()]
    act_last = [0, 0]
    x1f_last = [0, 0]

    def expert_pass(e_i, halves, w_first=None):
        cwb_ready = [None, None]
        for h in halves:
            def frow(e, h=h, e_i=e_i):
                return e.dma_start(rowE.ap()[0:1, :], cwT_h[h].ap()[e_i:e_i + 1, :])
            v = ev("sync", frow, waits=merge(cw_done[h], state["rowE_free"], w_first), inc="SX")
            def fcwb(e):
                return e.matmul(P[6].ap()[:, :], onesrow_r, rowE.ap()[0:1, :], start=True, stop=True)
            v = ev("tensor", fcwb, waits=merge(bankfree[6], {"SX": v}), inc="SP")
            state["rowE_free"] = {"SP": v}
            def fcwb2(e, h=h, e_i=e_i):
                return e.tensor_scalar(cwb_h[h].ap()[:, :], P[6].ap()[:, :], pes_c[:, e_i:e_i + 1],
                                       0.5, op0=ALU.mult, op1=ALU.mult)
            v = ev("vector", fcwb2, waits=merge({"SP": v}, w_sm, w_first), inc="SV")
            bankfree[6] = {"SV": v}
            cwb_ready[h] = {"SV": v}

        steps = [(h8, g) for h8 in range(8) for g in range(2)]
        gate_ready = {}
        gbase = state["glob_e"]

        def emit_gate_load(gs, guard=None):
            h8, g = steps[gs]
            buf = (gbase + gs) % 3
            def fgw(e, g=g, h8=h8, buf=buf, e_i=e_i):
                return e.dma_start(gateT.ap()[:, buf, :, :], gtw[e_i, g, h8])
            v = ev("gpsimd", fgw, waits=merge(gateT_free[buf], guard), inc="SW")
            gate_ready[gs] = ({"SW": v}, buf)
        for gs in range(3):
            emit_gate_load(gs, w_first)
        lin_ready = {}
        def emit_lin_load(m, guard=None):
            buf = m % 4
            def fln(e, m=m, buf=buf, e_i=e_i):
                return e.dma_start(linbb.ap()[:, buf, :, :], ltw[e_i, m])
            v = ev("gpsimd", fln, waits=merge(linb_free[buf], guard), inc="SW")
            lin_ready[m] = ({"SW": v}, buf)
        for m in range(4):
            emit_lin_load(m, w_first)
        first_evac = dict(w_first or {})
        first_gp = merge(w_first, xfeat_free)
        for gs, (h8, g) in enumerate(steps):
            gw_ready, buf = gate_ready.pop(gs)
            for m in range(2):
                for h in halves:
                    pi = state["gm1"] % 4
                    state["gm1"] += 1
                    pb = P[pi]
                    wb = merge(bankfree[pi], gw_ready, x1b_done[h])
                    for k in range(KC):
                        def fmm(e, buf=buf, m=m, k=k, pb=pb, h=h):
                            return e.matmul(pb.ap()[:, :],
                                            gateT.ap()[:, buf, k, m * 128:(m + 1) * 128],
                                            x1b_h[h].ap()[:, k, :],
                                            start=(k == 0), stop=(k == KC - 1))
                        v = ev("tensor", fmm, waits=wb if k == 0 else None,
                               inc="SP" if k == KC - 1 else None)
                    mm_done = v
                    tpar = (2 * m + h) % 2
                    t3 = T3.ap()[:, tpar, :]
                    t4 = T4.ap()[:, tpar, :]
                    slot = 2 * m + (h if len(halves) > 1 else halves[0] % 2)
                    if g == 0:
                        def f1(e, pb=pb, t3=t3):
                            return e.activation(t3, pb.ap()[:, :], AF.Square)
                        ev("scalar", f1, waits=merge({"SP": mm_done}, T3_free[tpar], first_evac))
                        def f2(e, t3=t3):
                            return e.activation(t3, t3, AF.Copy, scale=C3, bias=1.0)
                        va = ev("scalar", f2, inc="SA")
                        def f3(e, pb=pb, t3=t3, t4=t4):
                            return e.scalar_tensor_tensor(t4, pb.ap()[:, :], GC, t3,
                                                          op0=ALU.mult, op1=ALU.mult)
                        vv = ev("vector", f3, waits=merge({"SA": va}, T4_free[tpar], first_evac),
                                inc="SV")
                        T3_free[tpar] = {"SV": vv}
                        def f4(e, t4=t4):
                            return e.activation(t4, t4, AF.Tanh)
                        va = ev("scalar", f4, waits={"SV": vv}, inc="SA")
                        def f5(e, pb=pb, t4=t4, slot=slot):
                            return e.scalar_tensor_tensor(gelu0.ap()[:, slot, :], t4, 1.0,
                                                          pb.ap()[:, :], op0=ALU.add, op1=ALU.mult)
                        vv = ev("vector", f5, waits=merge({"SA": va}, gelu_free[slot], first_evac),
                                inc="SV")
                        T4_free[tpar] = {"SV": vv}
                        bankfree[pi] = {"SV": vv}
                    else:
                        def f6(e, pb=pb, t3=t3, h=h):
                            return e.tensor_mul(t3, pb.ap()[:, :], cwb_h[h].ap()[:, :])
                        vv = ev("vector", f6, waits=merge({"SP": mm_done}, cwb_ready[h],
                                                          T3_free[tpar], first_evac), inc="SV")
                        bankfree[pi] = {"SV": vv}
                        def f7(e, t3=t3, h8=h8, m=m, h=h, slot=slot):
                            return e.tensor_tensor(actb_h[h].ap()[:, h8 * 2 + m, :], t3,
                                                   gelu0.ap()[:, slot, :], op=ALU.mult)
                        vp = ev("gpsimd", f7, waits=merge({"SV": vv}, actb_free[h], first_gp),
                                inc="SL")
                        first_gp = None
                        actb_free[h] = {}
                        T3_free[tpar] = {"SL": vp}
                        gelu_free[slot] = {"SL": vp}
                        act_last[h] = vp
            first_evac = None
            gateT_free[buf] = {"SP": mm_done}
            if gs + 3 < len(steps):
                emit_gate_load(gs + 3)
        state["glob_e"] = (state["glob_e"] + len(steps)) % 3

        lr = 0
        lin_last_mm = [0, 0]
        for m in range(KC):
            lw, buf = lin_ready.pop(m)
            for h in halves:
                pb = P[4 + lr % 2]
                wb = merge(bankfree[4 + lr % 2], lw, {"SL": act_last[h]})
                for k in range(KC):
                    def fm2(e, m=m, k=k, pb=pb, buf=buf, h=h):
                        return e.matmul(pb.ap()[:, :], linbb.ap()[:, buf, k, :],
                                        actb_h[h].ap()[:, k, :], start=(k == 0), stop=(k == KC - 1))
                    v = ev("tensor", fm2, waits=wb if k == 0 else None,
                           inc="SP" if k == KC - 1 else None)
                mm2_done = v
                lin_last_mm[h] = v
                def fadd(e, m=m, pb=pb, h=h):
                    return e.tensor_add(x1f_h[h].ap()[:, m, :], x1f_h[h].ap()[:, m, :], pb.ap()[:, :])
                v = ev("vector", fadd, waits={"SP": mm2_done}, inc="SV")
                bankfree[4 + lr % 2] = {"SV": v}
                x1f_last[h] = v
                lr += 1
            linb_free[buf] = {"SP": mm2_done}
            if m + 4 < KC:
                emit_lin_load(m + 4)
        for h in halves:
            actb_free[h] = {"SP": lin_last_mm[h]}
            state["lin_last"][h] = lin_last_mm[h]

    for c in range(6):
        attn_chunk(c, None)
    r6 = attn_chunk(6, 0)
    r7 = attn_chunk(7, 1)
    ln1_h0 = layernorm(x1f0, 1e-5, {"SV": r6}, (sqbA, rowA, rowB, rowC), 4, 5, 6, 7)
    v = ev("scalar", lambda e: e.activation(
        x1b0.ap().rearrange("p k t -> p (k t)"), x1f0.ap().rearrange("p k t -> p (k t)"), AF.Copy),
        waits={"SV": ln1_h0}, inc="SA")
    x1b_done[0] = {"SA": v}
    router(x1f0, cwT0, ln1_h0, None)
    cw_done[0] = {"SV": cnt["SV"]}

    ln1_h1 = layernorm(x1f1, 1e-5, {"SV": r7, "SP": cnt["SP"]}, (sqbA, rowA, rowB, rowC), 4, 5, 6, 7)
    router(x1f1, cwT1, ln1_h1, merge(xtb_free))
    cw_done[1] = {"SV": cnt["SV"]}
    v = ev("scalar", lambda e: e.activation(
        x1b1.ap().rearrange("p k t -> p (k t)"), x1f1.ap().rearrange("p k t -> p (k t)"), AF.Copy),
        waits={"SV": cnt["SV"], "SP": cnt["SP"]}, inc="SA")
    x1b_done[1] = {"SA": v}

    w_first = merge(abuf_free, xtb_free, xfeat_free, {"SP": state["last_igw_mm"]})
    for e_i in range(NE):
        expert_pass(e_i, [0, 1], w_first=w_first if e_i == 0 else None)

    ln2_h0 = layernorm(x1f0, 1e-5,
                       merge({"SV": x1f_last[0]}, {"SP": state["lin_last"][0]}),
                       (sqbB, rowA2, rowB2, rowC2), 6, 7, 6, 7)
    vo = ev("sync", lambda e: e.dma_start(yout[0], x1f0.ap().bitcast(F32)[:, :, :]),
            waits={"SV": ln2_h0}, inc="SO")
    ln2_h1 = layernorm(x1f1, 1e-5,
                       merge({"SV": x1f_last[1]}, {"SP": cnt["SP"]}),
                       (sqbB, rowA2, rowB2, rowC2), 6, 7, 6, 7)
    vo = ev("sync", lambda e: e.dma_start(yout[1], x1f1.ap().bitcast(F32)[:, :, :]),
            waits={"SV": ln2_h1}, inc="SO")
    ev("sync", lambda e: e.nop(), waits={"SO": cnt["SO"], "SW": cnt["SW"], "SP": cnt["SP"],
                                         "SA": cnt["SA"], "SV": cnt["SV"], "SL": cnt["SL"],
                                         "SX": cnt["SX"]})

    def mk(name):
        def run(eng):
            done = {}
            for w, fn, inc in streams[name]:
                for s, val in w.items():
                    if done.get(s, -1) < val:
                        eng.wait_ge(SEMS[s], val)
                        done[s] = val
                ins = fn(eng)
                if inc:
                    ins.then_inc(SEMS[inc], 16 if inc in DMA_SEMS else 1)
        return run

    block.sync(mk("sync"))
    block.tensor(mk("tensor"))
    block.scalar(mk("scalar"))
    block.vector(mk("vector"))
    block.gpsimd(mk("gpsimd"))
    es.close()
    return nc



def _prep_inputs(ins):
    x = np.ascontiguousarray(ins["L0_attn_x"], dtype=np.float32)
    x_pre = np.ascontiguousarray(x.reshape(2, 8, T, KC, 128).transpose(0, 1, 4, 3, 2))
    ge = np.ascontiguousarray(ins["L0_ffn_gating_einsum"], dtype=np.float32)
    gtw = np.ascontiguousarray(
        ge.transpose(0, 1, 3, 2).reshape(NE, 2, KC, 128, 8, 256).transpose(0, 1, 4, 3, 2, 5))
    lin = np.ascontiguousarray(
        np.asarray(ins["L0_ffn_linear"], np.float32).reshape(NE, KC, 128, KC, 128)
        .transpose(0, 3, 2, 1, 4))
    igw = np.ascontiguousarray(ins["L0_attn_ig_w"], dtype=np.float32)
    agw = np.ascontiguousarray(ins["L0_attn_ag_w"], dtype=np.float32)

    sm32 = np.zeros((128, S32), np.float32)
    sm32[:, OFF_APAR:OFF_APAR + 16] = np.asarray(ins["L0_attn_a_param"], np.float32).reshape(16, 128).T
    sm32[:, OFF_IGB:OFF_IGB + 16] = np.asarray(ins["L0_attn_ig_b"], np.float32).T
    sm32[:, OFF_AGB:OFF_AGB + 16] = np.asarray(ins["L0_attn_ag_b"], np.float32).T
    sm32[:, OFF_RS:OFF_RS + 16] = np.asarray(ins["L0_ffn_router_scale"], np.float32).reshape(16, 128).T
    sm32[:, OFF_RL:OFF_RL + 128] = np.asarray(
        ins["L0_ffn_router_logits"], np.float32).reshape(16, 128, 8).transpose(1, 0, 2).reshape(128, 128)
    sm32[:, OFF_PES:OFF_PES + 8] = np.tile(
        np.asarray(ins["L0_ffn_per_expert_scale"], np.float32).reshape(1, 8), (128, 1))
    sm32[:, OFF_ID128:OFF_ID128 + 128] = np.eye(128, dtype=np.float32)
    sm32[0:8, OFF_ID8:OFF_ID8 + 8] = np.eye(8, dtype=np.float32)
    for i, val in enumerate(FCONSTS):
        sm32[:, OFF_FC + i] = np.float32(val)
    smr = np.ones((128, SR), np.float32)

    in_maps = []
    for j in range(NCORE):
        b, jm = j // 4, j % 4
        xc = np.zeros((NCHUNK, 128, KC, T), np.float32)
        nreal = 2 * jm + 2
        xc[NCHUNK - nreal:] = x_pre[b, :nreal]
        smc = sm32.copy()
        rst = np.zeros(8, np.float32)
        rst[NCHUNK - nreal] = 1.0
        smc[:, OFF_RSTA:OFF_RSTA + 8] = (1.0 - rst)[None, :]
        smc[:, OFF_RSTB:OFF_RSTB + 8] = rst[None, :]
        in_maps.append(dict(xin=xc, gtw=gtw, ltw=lin, igw_d=igw, agw_d=agw,
                            sm32_d=smc, smr_d=smr))
    return in_maps


def _unpack_out(results):
    out = np.empty((2, 4096, D), np.float32)
    for j in range(NCORE):
        b, jm = j // 4, j % 4
        arr = results[j]["yout"]
        for h in range(2):
            out[b, jm * 1024 + h * T: jm * 1024 + (h + 1) * T, :] = \
                arr[h].transpose(2, 1, 0).reshape(T, D)
    return out


_NC_CACHE = {}


def kernel(**inputs):
    if "prog" not in _NC_CACHE:
        _NC_CACHE["prog"] = build_program()
    nc = _NC_CACHE["prog"]
    in_maps = _prep_inputs(inputs)
    res = run_bass_kernel_spmd(nc, in_maps, core_ids=list(range(NCORE)))
    return _unpack_out(res.results)


# version 08
# speedup vs baseline: 1.0777x; 1.0777x over previous
import numpy as np
import ml_dtypes
from contextlib import ExitStack

import concourse.bass as bass
import concourse.mybir as mybir
from concourse.bass_utils import run_bass_kernel_spmd

F32 = mybir.dt.float32
F32R = mybir.dt.float32r
BF16 = mybir.dt.bfloat16
AF = mybir.ActivationFunctionType
ALU = mybir.AluOpType
AX = mybir.AxisListType.X

NCORE = 8
NCHUNK = 8
T = 512
KC = 16
NH = 16
NE = 8
D = 2048
GC = 0.7978845608028654
C3 = 0.044715

OFF_APAR, OFF_IGB, OFF_AGB, OFF_RS, OFF_RL, OFF_PES, OFF_ID128, OFF_ID8, OFF_RSTA, OFF_RSTB = (
    0, 16, 32, 48, 64, 192, 200, 328, 336, 344)
OFF_FC = 352
FCONSTS = (1.0, 1e-5, 1e-6, -0.5 * float(np.log(2048.0)), -1.0, 0.5, -0.5,
           0.044715, 0.7978845608028654, 2048.0)
S32 = OFF_FC + len(FCONSTS)
SR = 128


def build_program():
    nc = bass.Bass("TRN2", target_bir_lowering=False, detect_race_conditions=False)
    dt_f = nc.dram_tensor

    xin = dt_f("xin", [NCHUNK, 128, KC, T], F32R, kind="ExternalInput").ap()
    gtw = dt_f("gtw", [NE, 2, 8, 128, KC, 256], F32, kind="ExternalInput").ap()
    ltw = dt_f("ltw", [NE, KC, 128, KC, 128], F32, kind="ExternalInput").ap()
    igw_d = dt_f("igw_d", [NH, 128, 128], F32R, kind="ExternalInput").ap()
    agw_d = dt_f("agw_d", [NH, 128, 128], F32R, kind="ExternalInput").ap()
    sm32_d = dt_f("sm32_d", [128, S32], F32, kind="ExternalInput").ap()
    smr_d = dt_f("smr_d", [128, SR], F32R, kind="ExternalInput").ap()
    yout = dt_f("yout", [2, 128, KC, T], F32, kind="ExternalOutput").ap()

    KB = 1024
    BASE = 16512
    A = lambda name, shape, dtype, offset: nc.alloc_sbuf_tensor_at(
        name, shape, dtype, offset=BASE + offset)
    x1f0  = A("x1f0",  [128, KC, T], F32R, offset=0)
    ybuf  = A("ybuf",  [128, KC, T], F32,  offset=32 * KB)
    x1f1  = A("x1f1",  [128, KC, T], F32R, offset=32 * KB)
    x1b0  = A("x1b0",  [128, KC, T], BF16, offset=64 * KB)
    x1b1  = A("x1b1",  [128, KC, T], BF16, offset=80 * KB)
    sqbA  = A("sqbA",  [128, 8, T], BF16,  offset=80 * KB)
    lsb   = A("lsb",   [8, T], F32R,       offset=88 * KB)
    rowA  = A("rowA",  [1, T], F32R,       offset=90 * KB)
    rowB  = A("rowB",  [1, T], F32,        offset=92 * KB)
    rowC  = A("rowC",  [1, T], F32R,       offset=94 * KB)
    xfeat = A("xfeat", [128, KC, T], F32R, offset=96 * KB)
    actb0 = A("actb0", [128, KC, T], BF16, offset=96 * KB)
    actb1 = A("actb1", [128, KC, T], BF16, offset=112 * KB)
    sqbB  = A("sqbB",  [128, 8, T], BF16,  offset=96 * KB)
    rowA2 = A("rowA2", [1, T], F32R,       offset=104 * KB)
    rowB2 = A("rowB2", [1, T], F32,        offset=106 * KB)
    rowC2 = A("rowC2", [1, T], F32R,       offset=108 * KB)
    abuf  = A("abuf",  [128, KC, T], F32,  offset=128 * KB)
    gateT = A("gateT", [128, 3, KC, 256], BF16, offset=128 * KB)
    linbb = A("linbb", [128, 4, KC, 128], BF16, offset=152 * KB)
    xtb   = A("xtb",   [128, KC, T], F32,  offset=160 * KB)
    gelu0 = A("gelu0", [128, 4, T], BF16,  offset=168 * KB)
    T3    = A("T3",    [128, 2, T], F32,   offset=172 * KB)
    T4    = A("T4",    [128, 2, T], F32,   offset=176 * KB)
    cwb0  = A("cwb0",  [128, T], F32,      offset=180 * KB)
    cwb1  = A("cwb1",  [128, T], F32,      offset=182 * KB)
    rowE  = A("rowE",  [1, T], F32R,       offset=184 * KB)
    cwT1  = A("cwT1",  [8, T], F32R,       offset=186 * KB)
    PRM = 192 * KB
    igwin = A("igwin", [128, 4, 128], F32R, offset=PRM)
    agwin = A("agwin", [128, 4, 128], F32R, offset=PRM + 2 * KB)
    sm32  = A("sm32",  [128, S32], F32, offset=PRM + 4 * KB)
    smr   = A("smr",   [128, SR], F32R, offset=PRM + 4 * KB + 1472)
    cwT0  = A("cwT0",  [8, T], F32R,    offset=PRM + 6 * KB)
    nsp   = A("nsp",   [128, NH], F32,  offset=PRM + 8 * KB)
    spt   = A("spt",   [128, NH], F32,  offset=PRM + 8 * KB + 64)
    carry = A("carry", [128, KC], F32,  offset=PRM + 8 * KB + 128)
    ones_bf = A("ones_bf", [128, 1], BF16, offset=PRM + 8 * KB + 192)
    rlp   = A("rlp",   [128, KC, 8], F32R, offset=PRM + 8 * KB + 224)
    TIN = PRM + 8 * KB + 736
    cwsb  = A("cwsb",  [128, 8], F32,  offset=TIN)
    m1sb  = A("m1sb",  [128, 8], F32,  offset=TIN + 32)
    m2sb  = A("m2sb",  [128, 8], F32,  offset=TIN + 64)
    exsb  = A("exsb",  [128, 8], F32,  offset=TIN + 96)
    l2sb  = A("l2sb",  [128, 8], F32,  offset=TIN + 128)
    colA  = A("colA",  [128, 1], F32,  offset=TIN + 160)
    colB  = A("colB",  [128, 1], F32,  offset=TIN + 192)

    apar_c = sm32.ap()[:, OFF_APAR:OFF_APAR + 16]
    igb_c = sm32.ap()[:, OFF_IGB:OFF_IGB + 16]
    agb_c = sm32.ap()[:, OFF_AGB:OFF_AGB + 16]
    rs_c = sm32.ap()[:, OFF_RS:OFF_RS + 16]
    rl_c = sm32.ap()[:, OFF_RL:OFF_RL + 128]
    pes_c = sm32.ap()[:, OFF_PES:OFF_PES + 8]
    id128 = sm32.ap()[:, OFF_ID128:OFF_ID128 + 128]
    id8 = sm32.ap()[0:8, OFF_ID8:OFF_ID8 + 8]
    rsta_c = sm32.ap()[:, OFF_RSTA:OFF_RSTA + 8]
    rstb_c = sm32.ap()[:, OFF_RSTB:OFF_RSTB + 8]
    for i, val in enumerate(FCONSTS):
        nc.const_aps.register(F32, val, sm32.ap()[:, OFF_FC + i:OFF_FC + i + 1])
    onescol_r = smr.ap()[:, 0:1]
    onesrow_r = smr.ap()[0:1, 0:128]
    ones8row_r = smr.ap()[0:1, 0:8]

    es = ExitStack()
    P = [es.enter_context(nc.psum_tensor(f"P{i}", [128, T], F32)) for i in range(8)]
    SEMS = {}
    for s in ("SX", "SW", "SP", "SA", "SV", "SL", "SO"):
        SEMS[s] = es.enter_context(nc.semaphore(s))
    block = es.enter_context(nc.Block())

    DMA_SEMS = {"SX", "SW", "SO"}
    streams = {e: [] for e in ("sync", "tensor", "scalar", "vector", "gpsimd")}
    cnt = {s: 0 for s in SEMS}

    def ev(eng, fn, waits=None, inc=None):
        streams[eng].append((dict(waits or {}), fn, inc))
        if inc:
            cnt[inc] += 16 if inc in DMA_SEMS else 1
            return cnt[inc]
        return None

    def merge(*wds):
        out = {}
        for wd in wds:
            if not wd:
                continue
            for k, v in wd.items():
                out[k] = max(out.get(k, 0), v)
        return out

    w_sm = {}
    v = ev("sync", lambda e: e.dma_start(sm32.ap()[:, :], sm32_d), inc="SX")
    v = ev("sync", lambda e: e.dma_start(smr.ap()[:, :], smr_d), inc="SX")
    w_sm["SX"] = v

    ev("scalar", lambda e: e.activation(spt.ap()[:, :], apar_c, AF.Abs), waits=w_sm)
    ev("scalar", lambda e: e.activation(spt.ap()[:, :], spt.ap()[:, :], AF.Exp, scale=-1.0))
    v = ev("scalar", lambda e: e.activation(spt.ap()[:, :], spt.ap()[:, :], AF.Ln, bias=1.0), inc="SA")
    v = ev("vector", lambda e: e.tensor_scalar_max(nsp.ap()[:, :], apar_c, 0.0),
           waits=merge({"SA": v}, w_sm), inc="SV")
    v = ev("vector", lambda e: e.tensor_add(nsp.ap()[:, :], nsp.ap()[:, :], spt.ap()[:, :]),
           waits={"SV": v}, inc="SV")
    v = ev("vector", lambda e: e.tensor_scalar_mul(nsp.ap()[:, :], nsp.ap()[:, :], -8.0),
           waits={"SV": v}, inc="SV")
    for k in range(KC):
        def frl(e, k=k):
            return e.tensor_scalar_mul(rlp.ap()[:, k, :], rl_c[:, k * 8:(k + 1) * 8], rs_c[:, k:k + 1])
        v = ev("vector", frl, inc="SV")
    v = ev("vector", lambda e: e.memset(carry.ap()[:, :], 0.0), inc="SV")
    v = ev("vector", lambda e: e.memset(ones_bf.ap()[:, :], 1.0), inc="SV")
    w_setup_v = {"SV": cnt["SV"], "SA": cnt["SA"]}

    bankfree = [dict() for _ in range(8)]
    xfeat_free, abuf_free, xtb_free, ybuf_free = {}, {}, {}, {}
    gateT_free = [dict(), dict(), dict()]
    linb_free = [dict(), dict(), dict(), dict()]
    T3_free = [dict(), dict()]
    T4_free = [dict(), dict()]
    gelu_free = [dict() for _ in range(4)]
    actb_free = [dict(), dict()]
    igwin_free = [dict() for _ in range(4)]
    agwin_free = [dict() for _ in range(4)]
    state = {"carry_v": cnt["SV"], "gm1": 0, "wtile": 0, "glob_e": 0,
             "rowE_free": dict(), "last_igw_mm": 0, "lin_last": [0, 0],
             "scan_sv": [cnt["SV"]] * KC, "w7_sl": 0}

    def attn_chunk(cidx, half):
        w_x = {}
        v = ev("sync", lambda e, c=cidx: e.dma_start(xfeat.ap()[:, :, :], xin[c]),
               waits=merge(xfeat_free), inc="SX")
        w_x["SX"] = v
        ig_w = [None] * KC
        ag_w = [None] * KC
        def emit_win_load(hc):
            sl = state["wtile"] % 4
            state["wtile"] += 1
            def fli(e, hc=hc, sl=sl):
                return e.dma_start(igwin.ap()[:, sl, :], igw_d[hc])
            v = ev("gpsimd", fli, waits=igwin_free[sl], inc="SW")
            ig_w[hc] = ({"SW": v}, sl)
            def fla(e, hc=hc, sl=sl):
                return e.dma_start(agwin.ap()[:, sl, :], agw_d[hc])
            v = ev("gpsimd", fla, waits=agwin_free[sl], inc="SW")
            ag_w[hc] = ({"SW": v}, sl)
        for hc in range(4):
            emit_win_load(hc)
        ig_done = [0] * KC
        ag_done = [0] * KC
        sig_done = [0] * KC
        for hc in range(KC):
            pig, pag = P[2 * (hc % 2)], P[1 + 2 * (hc % 2)]
            wb = merge(w_x, bankfree[2 * (hc % 2)], bankfree[1 + 2 * (hc % 2)], w_setup_v)
            wi, sli = ig_w[hc]
            wa, sla = ag_w[hc]
            def fig(e, pig=pig, hc=hc, sli=sli):
                return e.matmul(pig.ap()[:, :], igwin.ap()[:, sli, :], xfeat.ap()[:, hc, :],
                                start=True, stop=True)
            ig_done[hc] = ev("tensor", fig, waits=merge(wb, wi), inc="SP")
            igwin_free[sli] = {"SP": ig_done[hc]}
            def fag(e, pag=pag, hc=hc, sla=sla):
                return e.matmul(pag.ap()[:, :], agwin.ap()[:, sla, :], xfeat.ap()[:, hc, :],
                                start=True, stop=True)
            ag_done[hc] = ev("tensor", fag, waits=wa, inc="SP")
            agwin_free[sla] = {"SP": ag_done[hc]}
            if hc + 4 < KC:
                emit_win_load(hc + 4)
            def fsx(e, pig=pig, hc=hc):
                return e.activation(xtb.ap()[:, hc, :], pig.ap()[:, :], AF.Sigmoid,
                                    bias=igb_c[:, hc:hc + 1])
            ev("scalar", fsx, waits=merge({"SP": ig_done[hc]},
                                          {"SV": state["scan_sv"][hc]}))
            def fsa(e, pag=pag, hc=hc):
                return e.activation(abuf.ap()[:, hc, :], pag.ap()[:, :], AF.Sigmoid,
                                    bias=agb_c[:, hc:hc + 1])
            sig_done[hc] = ev("scalar", fsa, waits={"SP": ag_done[hc]}, inc="SA")
            bankfree[2 * (hc % 2)] = {"SA": sig_done[hc]}
            bankfree[1 + 2 * (hc % 2)] = {"SA": sig_done[hc]}
        state["last_igw_mm"] = ag_done[KC - 1]
        a_exp = 0
        for hc in range(KC):
            def fexp(e, hc=hc):
                return e.activation(abuf.ap()[:, hc, :], abuf.ap()[:, hc, :], AF.Exp,
                                    scale=nsp.ap()[:, hc:hc + 1])
            a_exp = ev("scalar", fexp, waits={"SA": sig_done[0]} if hc == 0 else None, inc="SA")
        af = abuf.ap().rearrange("p k t -> p (k t)")
        yf = ybuf.ap().rearrange("p k t -> p (k t)")
        xtf = xtb.ap().rearrange("p k t -> p (k t)")
        xff = xfeat.ap().rearrange("p k t -> p (k t)")
        v6 = ev("gpsimd", lambda e: e.tensor_mul(xtf, xtf, xff),
                waits=merge({"SA": sig_done[KC - 1]}, {"SV": state["carry_v"]},
                            w_x), inc="SL")
        xt1_last = v6
        v2 = ev("vector", lambda e: e.tensor_mul(yf, af, af),
                waits=merge({"SA": a_exp}, {"SL": state["w7_sl"]}), inc="SV")
        v3 = ev("scalar", lambda e: e.activation(yf, yf, AF.Sqrt, scale=-1.0, bias=1.0),
                waits={"SV": v2}, inc="SA")
        def frst(e, c=cidx):
            return e.tensor_scalar(ybuf.ap()[:, :, 0:1], ybuf.ap()[:, :, 0:1],
                                   rsta_c[:, c:c + 1], rstb_c[:, c:c + 1],
                                   op0=ALU.mult, op1=ALU.add)
        v5 = ev("vector", frst, waits={"SA": v3}, inc="SV")
        v7 = ev("gpsimd", lambda e: e.tensor_mul(xtf, xtf, yf),
                waits={"SA": v3, "SV": v5}, inc="SL")
        state["w7_sl"] = v7
        for hc in range(KC):
            def fsc(e, hc=hc):
                return e.tensor_tensor_scan(ybuf.ap()[:, hc, :], abuf.ap()[:, hc, :],
                                            xtb.ap()[:, hc, :], carry.ap()[:, hc:hc + 1],
                                            op0=ALU.mult, op1=ALU.add)
            v = ev("vector", fsc, waits={"SL": v7} if hc == 0 else None, inc="SV")
            state["scan_sv"][hc] = v
        def fcu(e):
            return e.tensor_copy(carry.ap()[:, :], ybuf.ap()[:, :, T - 1:T])
        v = ev("vector", fcu, waits={"SV": v}, inc="SV")
        state["carry_v"] = v
        if half is not None:
            dst = x1f0 if half == 0 else x1f1
            def fr(e, dst=dst):
                return e.tensor_add(dst.ap().rearrange("p k t -> p (k t)"), xff, yf)
            v = ev("vector", fr, inc="SV")
            xfeat_free.clear()
            xfeat_free["SV"] = v
        else:
            xfeat_free.clear()
            xfeat_free["SL"] = xt1_last
            xfeat_free["SP"] = ag_done[KC - 1]
        abuf_free.clear()
        abuf_free["SV"] = v
        xtb_free.clear()
        xtb_free["SV"] = v
        ybuf_free.clear()
        ybuf_free["SV"] = v
        return v

    def layernorm(x1, eps, wpre, scr, bs0, bs1, bb0, bb1):
        sqb, rA, rB, rC = scr
        sq_done = [[0] * 8, [0] * 8]
        pe_wave = [0, 0]
        for wv in range(2):
            for i in range(8):
                hc = wv * 8 + i
                wq = wpre if (wv == 0 and i == 0) else None
                if wv == 1:
                    wq = merge(wq, {"SP": pe_wave[0]})
                def fq(e, hc=hc, i=i):
                    return e.activation(sqb.ap()[:, i, :], x1.ap()[:, hc, :], AF.Square)
                sq_done[wv][i] = ev("scalar", fq, waits=wq, inc="SA")
            for i in range(8):
                hc = wv * 8 + i
                w0 = merge(bankfree[bs1], {"SA": sq_done[wv][i]}, wpre) if (wv == 0 and i == 0) \
                    else {"SA": sq_done[wv][i]}
                def fs2(e, i=i, hc=hc):
                    return e.matmul(P[bs1].ap()[0:1, :], ones_bf.ap()[:, :], sqb.ap()[:, i, :],
                                    start=(hc == 0), stop=(hc == KC - 1))
                pe_wave[wv] = ev("tensor", fs2, waits=w0, inc="SP")
        sq_sum_done = pe_wave[1]
        for hc in range(KC):
            def fs(e, hc=hc):
                return e.matmul(P[bs0].ap()[0:1, :], onescol_r, x1.ap()[:, hc, :],
                                start=(hc == 0), stop=(hc == KC - 1))
            v = ev("tensor", fs, waits=merge(bankfree[bs0], wpre, w_sm) if hc == 0 else None,
                   inc="SP" if hc == KC - 1 else None)
        sum_done = v
        v = ev("vector", lambda e: e.tensor_scalar_mul(rA.ap()[:, :], P[bs0].ap()[0:1, :], 1.0 / D),
               waits=merge({"SP": sum_done}, wpre), inc="SV")
        v = ev("vector", lambda e: e.scalar_tensor_tensor(rB.ap()[:, :], rA.ap()[:, :], -1.0,
                                                          rA.ap()[:, :], op0=ALU.mult, op1=ALU.mult),
               waits=merge({"SV": v}, wpre), inc="SV")
        v = ev("vector", lambda e: e.scalar_tensor_tensor(rB.ap()[:, :], P[bs1].ap()[0:1, :], 1.0 / D,
                                                          rB.ap()[:, :], op0=ALU.mult, op1=ALU.add),
               waits={"SP": sq_sum_done, "SV": v}, inc="SV")
        bankfree[bs0] = {"SV": v}
        bankfree[bs1] = {"SV": v}
        ev("scalar", lambda e: e.activation(rB.ap()[:, :], rB.ap()[:, :], AF.Ln, bias=eps),
           waits={"SV": v})
        va = ev("scalar", lambda e: e.activation(rC.ap()[:, :], rB.ap()[:, :], AF.Exp, scale=-0.5),
                inc="SA")
        v2 = ev("vector", lambda e: e.scalar_tensor_tensor(rA.ap()[:, :], rA.ap()[:, :], -1.0,
                                                           rC.ap()[:, :], op0=ALU.mult, op1=ALU.mult),
                waits={"SA": va}, inc="SV")
        vb1 = ev("tensor", lambda e: e.matmul(P[bb0].ap()[:, :], onesrow_r, rC.ap()[:, :],
                                              start=True, stop=True),
                 waits=merge(bankfree[bb0], {"SA": va}), inc="SP")
        vb2 = ev("tensor", lambda e: e.matmul(P[bb1].ap()[:, :], onesrow_r, rA.ap()[:, :],
                                              start=True, stop=True),
                 waits=merge(bankfree[bb1], {"SV": v2}), inc="SP")
        for hc in range(KC):
            def fa1(e, hc=hc):
                return e.tensor_mul(x1.ap()[:, hc, :], x1.ap()[:, hc, :], P[bb0].ap()[:, :])
            ev("vector", fa1, waits={"SP": max(vb1, vb2)} if hc == 0 else None)
            def fa2(e, hc=hc):
                return e.tensor_add(x1.ap()[:, hc, :], x1.ap()[:, hc, :], P[bb1].ap()[:, :])
            v = ev("vector", fa2, inc="SV")
        bankfree[bb0] = {"SV": v}
        bankfree[bb1] = {"SV": v}
        return v

    def router(x1, cwT, ln_done, w_extra):
        sqr_done = [[0] * 8, [0] * 8]
        pe_wave = [0, 0]
        for wv in range(2):
            for i in range(8):
                hc = wv * 8 + i
                wq = {"SV": ln_done} if (wv == 0 and i == 0) else None
                if wv == 1:
                    wq = merge(wq, {"SP": pe_wave[0]})
                def fq(e, hc=hc, i=i):
                    return e.activation(sqbA.ap()[:, i, :], x1.ap()[:, hc, :], AF.Square)
                sqr_done[wv][i] = ev("scalar", fq, waits=wq, inc="SA")
            for i in range(8):
                hc = wv * 8 + i
                w0 = merge(bankfree[5], {"SA": sqr_done[wv][i]}) if (wv == 0 and i == 0) \
                    else {"SA": sqr_done[wv][i]}
                def fs2(e, i=i, hc=hc):
                    return e.matmul(P[5].ap()[0:1, :], ones_bf.ap()[:, :], sqbA.ap()[:, i, :],
                                    start=(hc == 0), stop=(hc == KC - 1))
                pe_wave[wv] = ev("tensor", fs2, waits=w0, inc="SP")
        v = ev("vector", lambda e: e.tensor_scalar_mul(rowB.ap()[:, :], P[5].ap()[0:1, :], 1.0 / D),
               waits={"SP": pe_wave[1]}, inc="SV")
        bankfree[5] = {"SV": v}
        ev("scalar", lambda e: e.activation(rowB.ap()[:, :], rowB.ap()[:, :], AF.Ln, bias=1e-6),
           waits={"SV": v})
        sc_done = ev("scalar", lambda e: e.activation(rowC.ap()[:, :], rowB.ap()[:, :], AF.Exp,
                                                      scale=-0.5, bias=-0.5 * float(np.log(D))),
                     inc="SA")
        for hc in range(KC):
            def fl(e, hc=hc):
                return e.matmul(P[4].ap()[0:8, :], rlp.ap()[:, hc, :], x1.ap()[:, hc, :],
                                start=(hc == 0), stop=(hc == KC - 1))
            v = ev("tensor", fl, waits=merge(bankfree[4], {"SV": ln_done}) if hc == 0 else None,
                   inc="SP" if hc == KC - 1 else None)
        logits_done = v
        scb_done = ev("tensor", lambda e: e.matmul(P[5].ap()[0:8, :], ones8row_r, rowC.ap()[:, :],
                                                   start=True, stop=True),
                      waits=merge(bankfree[5], {"SA": sc_done}), inc="SP")
        v = ev("vector", lambda e: e.tensor_copy(lsb.ap()[:, :], P[4].ap()[0:8, :]),
               waits={"SP": logits_done}, inc="SV")
        bankfree[4] = {"SV": v}
        v = ev("vector", lambda e: e.tensor_mul(lsb.ap()[:, :], lsb.ap()[:, :], P[5].ap()[0:8, :]),
               waits={"SP": scb_done}, inc="SV")
        bankfree[5] = {"SV": v}
        lsb_done = v
        for tc in range(4):
            def ftr(e, tc=tc):
                return e.transpose(P[6].ap()[:, 0:8], lsb.ap().bitcast(F32)[0:8, tc * 128:(tc + 1) * 128], id8)
            v = ev("tensor", ftr, waits=merge(bankfree[6], {"SV": lsb_done}, w_sm), inc="SP")
            v = ev("vector", lambda e: e.tensor_copy(cwsb.ap()[:, :], P[6].ap()[:, 0:8]),
                   waits={"SP": v}, inc="SV")
            bankfree[6] = {"SV": v}
            v = ev("vector", lambda e: e.reduce_max(colA.ap()[:, :], cwsb.ap()[:, :], axis=AX,
                                                    negate=True), waits={"SV": v}, inc="SV")
            v = ev("vector", lambda e: e.tensor_scalar(m1sb.ap()[:, :], cwsb.ap()[:, :],
                                                       colA.ap()[:, 0:1], 0.0,
                                                       op0=ALU.add, op1=ALU.is_ge),
                   waits={"SV": v}, inc="SV")
            vex = ev("scalar", lambda e: e.activation(exsb.ap()[:, :], cwsb.ap()[:, :], AF.Exp,
                                                      bias=colA.ap()[:, 0:1]),
                     waits={"SV": v}, inc="SA")
            v = ev("vector", lambda e: e.scalar_tensor_tensor(l2sb.ap()[:, :], m1sb.ap()[:, :], -1e30,
                                                              cwsb.ap()[:, :], op0=ALU.mult,
                                                              op1=ALU.add), waits={"SV": v}, inc="SV")
            v = ev("vector", lambda e: e.reduce_max(colB.ap()[:, :], l2sb.ap()[:, :], axis=AX,
                                                    negate=True), waits={"SV": v}, inc="SV")
            v = ev("vector", lambda e: e.tensor_scalar(m2sb.ap()[:, :], l2sb.ap()[:, :],
                                                       colB.ap()[:, 0:1], 0.0,
                                                       op0=ALU.add, op1=ALU.is_ge),
                   waits={"SV": v}, inc="SV")
            v = ev("vector", lambda e: e.tensor_add(m1sb.ap()[:, :], m1sb.ap()[:, :], m2sb.ap()[:, :]),
                   waits={"SV": v}, inc="SV")
            v = ev("vector", lambda e: e.tensor_mul(m1sb.ap()[:, :], m1sb.ap()[:, :], exsb.ap()[:, :]),
                   waits={"SA": vex, "SV": v}, inc="SV")
            v = ev("vector", lambda e: e.reduce_sum(colB.ap()[:, :], m1sb.ap()[:, :], axis=AX),
                   waits={"SV": v}, inc="SV")
            v = ev("vector", lambda e: e.reciprocal(colA.ap()[:, :], colB.ap()[:, :]),
                   waits={"SV": v}, inc="SV")
            v = ev("vector", lambda e: e.tensor_scalar_mul(cwsb.ap()[:, :], m1sb.ap()[:, :],
                                                           colA.ap()[:, 0:1]),
                   waits={"SV": v}, inc="SV")
            def ftr2(e, tc=tc):
                return e.transpose(P[7].ap()[0:8, 0:128], cwsb.ap()[:, 0:8], id128)
            v = ev("tensor", ftr2, waits=merge(bankfree[7], {"SV": v}), inc="SP")
            def fcp(e, tc=tc):
                return e.tensor_copy(cwT.ap()[:, tc * 128:(tc + 1) * 128], P[7].ap()[0:8, 0:128])
            v = ev("vector", fcp, waits=merge({"SP": v}, w_extra if tc == 0 else None), inc="SV")
            bankfree[7] = {"SV": v}
        return cnt["SV"]

    x1f_h = [x1f0, x1f1]
    x1b_h = [x1b0, x1b1]
    actb_h = [actb0, actb1]
    cwb_h = [cwb0, cwb1]
    cwT_h = [cwT0, cwT1]
    cw_done = [dict(), dict()]
    x1b_done = [dict(), dict()]
    act_last = [0, 0]
    x1f_last = [0, 0]

    def expert_pass(e_i, halves, w_first=None):
        cwb_ready = [None, None]
        for h in halves:
            def frow(e, h=h, e_i=e_i):
                return e.dma_start(rowE.ap()[0:1, :], cwT_h[h].ap()[e_i:e_i + 1, :])
            v = ev("sync", frow, waits=merge(cw_done[h], state["rowE_free"], w_first), inc="SX")
            def fcwb(e):
                return e.matmul(P[6].ap()[:, :], onesrow_r, rowE.ap()[0:1, :], start=True, stop=True)
            v = ev("tensor", fcwb, waits=merge(bankfree[6], {"SX": v}), inc="SP")
            state["rowE_free"] = {"SP": v}
            def fcwb2(e, h=h, e_i=e_i):
                return e.tensor_scalar(cwb_h[h].ap()[:, :], P[6].ap()[:, :], pes_c[:, e_i:e_i + 1],
                                       0.5, op0=ALU.mult, op1=ALU.mult)
            v = ev("vector", fcwb2, waits=merge({"SP": v}, w_sm, w_first), inc="SV")
            bankfree[6] = {"SV": v}
            cwb_ready[h] = {"SV": v}

        steps = [(h8, g) for h8 in range(8) for g in range(2)]
        gate_ready = {}
        gbase = state["glob_e"]

        def emit_gate_load(gs, guard=None):
            h8, g = steps[gs]
            buf = (gbase + gs) % 3
            def fgw(e, g=g, h8=h8, buf=buf, e_i=e_i):
                return e.dma_start(gateT.ap()[:, buf, :, :], gtw[e_i, g, h8])
            v = ev("gpsimd", fgw, waits=merge(gateT_free[buf], guard), inc="SW")
            gate_ready[gs] = ({"SW": v}, buf)
        for gs in range(3):
            emit_gate_load(gs, w_first)
        lin_ready = {}
        def emit_lin_load(m, guard=None):
            buf = m % 4
            def fln(e, m=m, buf=buf, e_i=e_i):
                return e.dma_start(linbb.ap()[:, buf, :, :], ltw[e_i, m])
            v = ev("gpsimd", fln, waits=merge(linb_free[buf], guard), inc="SW")
            lin_ready[m] = ({"SW": v}, buf)
        for m in range(4):
            emit_lin_load(m, w_first)
        first_evac = dict(w_first or {})
        first_gp = merge(w_first, xfeat_free)
        for gs, (h8, g) in enumerate(steps):
            gw_ready, buf = gate_ready.pop(gs)
            for m in range(2):
                for h in halves:
                    pi = state["gm1"] % 4
                    state["gm1"] += 1
                    pb = P[pi]
                    wb = merge(bankfree[pi], gw_ready, x1b_done[h])
                    for k in range(KC):
                        def fmm(e, buf=buf, m=m, k=k, pb=pb, h=h):
                            return e.matmul(pb.ap()[:, :],
                                            gateT.ap()[:, buf, k, m * 128:(m + 1) * 128],
                                            x1b_h[h].ap()[:, k, :],
                                            start=(k == 0), stop=(k == KC - 1))
                        v = ev("tensor", fmm, waits=wb if k == 0 else None,
                               inc="SP" if k == KC - 1 else None)
                    mm_done = v
                    tpar = (2 * m + h) % 2
                    t3 = T3.ap()[:, tpar, :]
                    t4 = T4.ap()[:, tpar, :]
                    slot = 2 * m + (h if len(halves) > 1 else halves[0] % 2)
                    if g == 0:
                        def f1(e, pb=pb, t3=t3):
                            return e.activation(t3, pb.ap()[:, :], AF.Square)
                        ev("scalar", f1, waits=merge({"SP": mm_done}, T3_free[tpar], first_evac))
                        def f2(e, t3=t3):
                            return e.activation(t3, t3, AF.Copy, scale=C3, bias=1.0)
                        va = ev("scalar", f2, inc="SA")
                        def f3(e, pb=pb, t3=t3, t4=t4):
                            return e.scalar_tensor_tensor(t4, pb.ap()[:, :], GC, t3,
                                                          op0=ALU.mult, op1=ALU.mult)
                        vv = ev("vector", f3, waits=merge({"SA": va}, T4_free[tpar], first_evac),
                                inc="SV")
                        T3_free[tpar] = {"SV": vv}
                        def f4(e, t4=t4):
                            return e.activation(t4, t4, AF.Tanh)
                        va = ev("scalar", f4, waits={"SV": vv}, inc="SA")
                        def f5(e, pb=pb, t4=t4, slot=slot):
                            return e.scalar_tensor_tensor(gelu0.ap()[:, slot, :], t4, 1.0,
                                                          pb.ap()[:, :], op0=ALU.add, op1=ALU.mult)
                        vv = ev("vector", f5, waits=merge({"SA": va}, gelu_free[slot], first_evac),
                                inc="SV")
                        T4_free[tpar] = {"SV": vv}
                        bankfree[pi] = {"SV": vv}
                    else:
                        def f6(e, pb=pb, t3=t3, h=h):
                            return e.tensor_mul(t3, pb.ap()[:, :], cwb_h[h].ap()[:, :])
                        vv = ev("vector", f6, waits=merge({"SP": mm_done}, cwb_ready[h],
                                                          T3_free[tpar], first_evac), inc="SV")
                        bankfree[pi] = {"SV": vv}
                        def f7(e, t3=t3, h8=h8, m=m, h=h, slot=slot):
                            return e.tensor_tensor(actb_h[h].ap()[:, h8 * 2 + m, :], t3,
                                                   gelu0.ap()[:, slot, :], op=ALU.mult)
                        vp = ev("gpsimd", f7, waits=merge({"SV": vv}, actb_free[h], first_gp),
                                inc="SL")
                        first_gp = None
                        actb_free[h] = {}
                        T3_free[tpar] = {"SL": vp}
                        gelu_free[slot] = {"SL": vp}
                        act_last[h] = vp
            first_evac = None
            gateT_free[buf] = {"SP": mm_done}
            if gs + 3 < len(steps):
                emit_gate_load(gs + 3)
        state["glob_e"] = (state["glob_e"] + len(steps)) % 3

        lr = 0
        lin_last_mm = [0, 0]
        for m in range(KC):
            lw, buf = lin_ready.pop(m)
            for h in halves:
                pb = P[4 + lr % 2]
                wb = merge(bankfree[4 + lr % 2], lw, {"SL": act_last[h]})
                for k in range(KC):
                    def fm2(e, m=m, k=k, pb=pb, buf=buf, h=h):
                        return e.matmul(pb.ap()[:, :], linbb.ap()[:, buf, k, :],
                                        actb_h[h].ap()[:, k, :], start=(k == 0), stop=(k == KC - 1))
                    v = ev("tensor", fm2, waits=wb if k == 0 else None,
                           inc="SP" if k == KC - 1 else None)
                mm2_done = v
                lin_last_mm[h] = v
                def fadd(e, m=m, pb=pb, h=h):
                    return e.tensor_add(x1f_h[h].ap()[:, m, :], x1f_h[h].ap()[:, m, :], pb.ap()[:, :])
                v = ev("vector", fadd, waits={"SP": mm2_done}, inc="SV")
                bankfree[4 + lr % 2] = {"SV": v}
                x1f_last[h] = v
                lr += 1
            linb_free[buf] = {"SP": mm2_done}
            if m + 4 < KC:
                emit_lin_load(m + 4)
        for h in halves:
            actb_free[h] = {"SP": lin_last_mm[h]}
            state["lin_last"][h] = lin_last_mm[h]

    for c in range(6):
        attn_chunk(c, None)
    r6 = attn_chunk(6, 0)
    r7 = attn_chunk(7, 1)
    ln1_h0 = layernorm(x1f0, 1e-5, {"SV": r6}, (sqbA, rowA, rowB, rowC), 4, 5, 6, 7)
    v = ev("scalar", lambda e: e.activation(
        x1b0.ap().rearrange("p k t -> p (k t)"), x1f0.ap().rearrange("p k t -> p (k t)"), AF.Copy),
        waits={"SV": ln1_h0}, inc="SA")
    x1b_done[0] = {"SA": v}
    router(x1f0, cwT0, ln1_h0, None)
    cw_done[0] = {"SV": cnt["SV"]}

    ln1_h1 = layernorm(x1f1, 1e-5, {"SV": r7, "SP": cnt["SP"]}, (sqbA, rowA, rowB, rowC), 4, 5, 6, 7)
    router(x1f1, cwT1, ln1_h1, merge(xtb_free))
    cw_done[1] = {"SV": cnt["SV"]}
    v = ev("scalar", lambda e: e.activation(
        x1b1.ap().rearrange("p k t -> p (k t)"), x1f1.ap().rearrange("p k t -> p (k t)"), AF.Copy),
        waits={"SV": cnt["SV"], "SP": cnt["SP"]}, inc="SA")
    x1b_done[1] = {"SA": v}

    w_first = merge(abuf_free, xtb_free, xfeat_free, {"SP": state["last_igw_mm"]})
    for e_i in range(NE):
        expert_pass(e_i, [0, 1], w_first=w_first if e_i == 0 else None)

    ln2_h0 = layernorm(x1f0, 1e-5,
                       merge({"SV": x1f_last[0]}, {"SP": state["lin_last"][0]}),
                       (sqbB, rowA2, rowB2, rowC2), 6, 7, 6, 7)
    vo = ev("sync", lambda e: e.dma_start(yout[0], x1f0.ap().bitcast(F32)[:, :, :]),
            waits={"SV": ln2_h0}, inc="SO")
    ln2_h1 = layernorm(x1f1, 1e-5,
                       merge({"SV": x1f_last[1]}, {"SP": cnt["SP"]}),
                       (sqbB, rowA2, rowB2, rowC2), 6, 7, 6, 7)
    vo = ev("sync", lambda e: e.dma_start(yout[1], x1f1.ap().bitcast(F32)[:, :, :]),
            waits={"SV": ln2_h1}, inc="SO")
    ev("sync", lambda e: e.nop(), waits={"SO": cnt["SO"], "SW": cnt["SW"], "SP": cnt["SP"],
                                         "SA": cnt["SA"], "SV": cnt["SV"], "SL": cnt["SL"],
                                         "SX": cnt["SX"]})

    def mk(name):
        def run(eng):
            done = {}
            for w, fn, inc in streams[name]:
                for s, val in w.items():
                    if done.get(s, -1) < val:
                        eng.wait_ge(SEMS[s], val)
                        done[s] = val
                ins = fn(eng)
                if inc:
                    ins.then_inc(SEMS[inc], 16 if inc in DMA_SEMS else 1)
        return run

    block.sync(mk("sync"))
    block.tensor(mk("tensor"))
    block.scalar(mk("scalar"))
    block.vector(mk("vector"))
    block.gpsimd(mk("gpsimd"))
    es.close()
    return nc



def _prep_inputs(ins):
    x = np.ascontiguousarray(ins["L0_attn_x"], dtype=np.float32)
    x_pre = np.ascontiguousarray(x.reshape(2, 8, T, KC, 128).transpose(0, 1, 4, 3, 2))
    ge = np.ascontiguousarray(ins["L0_ffn_gating_einsum"], dtype=np.float32)
    gtw = np.ascontiguousarray(
        ge.transpose(0, 1, 3, 2).reshape(NE, 2, KC, 128, 8, 256).transpose(0, 1, 4, 3, 2, 5))
    lin = np.ascontiguousarray(
        np.asarray(ins["L0_ffn_linear"], np.float32).reshape(NE, KC, 128, KC, 128)
        .transpose(0, 3, 2, 1, 4))
    igw = np.ascontiguousarray(ins["L0_attn_ig_w"], dtype=np.float32)
    agw = np.ascontiguousarray(ins["L0_attn_ag_w"], dtype=np.float32)

    sm32 = np.zeros((128, S32), np.float32)
    sm32[:, OFF_APAR:OFF_APAR + 16] = np.asarray(ins["L0_attn_a_param"], np.float32).reshape(16, 128).T
    sm32[:, OFF_IGB:OFF_IGB + 16] = np.asarray(ins["L0_attn_ig_b"], np.float32).T
    sm32[:, OFF_AGB:OFF_AGB + 16] = np.asarray(ins["L0_attn_ag_b"], np.float32).T
    sm32[:, OFF_RS:OFF_RS + 16] = np.asarray(ins["L0_ffn_router_scale"], np.float32).reshape(16, 128).T
    sm32[:, OFF_RL:OFF_RL + 128] = np.asarray(
        ins["L0_ffn_router_logits"], np.float32).reshape(16, 128, 8).transpose(1, 0, 2).reshape(128, 128)
    sm32[:, OFF_PES:OFF_PES + 8] = np.tile(
        np.asarray(ins["L0_ffn_per_expert_scale"], np.float32).reshape(1, 8), (128, 1))
    sm32[:, OFF_ID128:OFF_ID128 + 128] = np.eye(128, dtype=np.float32)
    sm32[0:8, OFF_ID8:OFF_ID8 + 8] = np.eye(8, dtype=np.float32)
    for i, val in enumerate(FCONSTS):
        sm32[:, OFF_FC + i] = np.float32(val)
    smr = np.ones((128, SR), np.float32)

    in_maps = []
    for j in range(NCORE):
        b, jm = j // 4, j % 4
        xc = np.zeros((NCHUNK, 128, KC, T), np.float32)
        nreal = 2 * jm + 2
        xc[NCHUNK - nreal:] = x_pre[b, :nreal]
        smc = sm32.copy()
        rst = np.zeros(8, np.float32)
        rst[NCHUNK - nreal] = 1.0
        smc[:, OFF_RSTA:OFF_RSTA + 8] = (1.0 - rst)[None, :]
        smc[:, OFF_RSTB:OFF_RSTB + 8] = rst[None, :]
        in_maps.append(dict(xin=xc, gtw=gtw, ltw=lin, igw_d=igw, agw_d=agw,
                            sm32_d=smc, smr_d=smr))
    return in_maps


def _unpack_out(results):
    out = np.empty((2, 4096, D), np.float32)
    for j in range(NCORE):
        b, jm = j // 4, j % 4
        arr = results[j]["yout"]
        for h in range(2):
            out[b, jm * 1024 + h * T: jm * 1024 + (h + 1) * T, :] = \
                arr[h].transpose(2, 1, 0).reshape(T, D)
    return out


_NC_CACHE = {}


def kernel(**inputs):
    if "prog" not in _NC_CACHE:
        _NC_CACHE["prog"] = build_program()
    nc = _NC_CACHE["prog"]
    in_maps = _prep_inputs(inputs)
    res = run_bass_kernel_spmd(nc, in_maps, core_ids=list(range(NCORE)))
    return _unpack_out(res.results)


# version 09 (final)
# speedup vs baseline: 1.1607x; 1.0770x over previous
import numpy as np
import ml_dtypes
from contextlib import ExitStack

import concourse.bass as bass
import concourse.mybir as mybir
from concourse.bass_utils import run_bass_kernel_spmd

F32 = mybir.dt.float32
F32R = mybir.dt.float32r
BF16 = mybir.dt.bfloat16
AF = mybir.ActivationFunctionType
ALU = mybir.AluOpType
AX = mybir.AxisListType.X

NCORE = 8
NCHUNK = 8
T = 512
KC = 16
NH = 16
NE = 8
D = 2048
GC = 0.7978845608028654
C3 = 0.044715

OFF_APAR, OFF_IGB, OFF_AGB, OFF_RS, OFF_RL, OFF_PES, OFF_ID128, OFF_ID8, OFF_RSTA, OFF_RSTB = (
    0, 16, 32, 48, 64, 192, 200, 328, 336, 344)
OFF_FC = 352
FCONSTS = (1.0, 1e-5, 1e-6, -0.5 * float(np.log(2048.0)), -1.0, 0.5, -0.5,
           0.044715, 0.7978845608028654, 2048.0)
S32 = OFF_FC + len(FCONSTS)
SR = 128


def build_program():
    nc = bass.Bass("TRN2", target_bir_lowering=False, detect_race_conditions=False)
    dt_f = nc.dram_tensor

    xin = dt_f("xin", [NCHUNK, 128, KC, T], F32R, kind="ExternalInput").ap()
    gtw = dt_f("gtw", [NE, 2, 8, 128, KC, 256], F32, kind="ExternalInput").ap()
    ltw = dt_f("ltw", [NE, KC, 128, KC, 128], F32, kind="ExternalInput").ap()
    igw_d = dt_f("igw_d", [NH, 128, 128], F32R, kind="ExternalInput").ap()
    agw_d = dt_f("agw_d", [NH, 128, 128], F32R, kind="ExternalInput").ap()
    sm32_d = dt_f("sm32_d", [128, S32], F32, kind="ExternalInput").ap()
    smr_d = dt_f("smr_d", [128, SR], F32R, kind="ExternalInput").ap()
    yout = dt_f("yout", [2, 128, KC, T], F32, kind="ExternalOutput").ap()

    KB = 1024
    BASE = 16512
    A = lambda name, shape, dtype, offset: nc.alloc_sbuf_tensor_at(
        name, shape, dtype, offset=BASE + offset)
    x1f0  = A("x1f0",  [128, KC, T], F32R, offset=0)
    ybuf  = A("ybuf",  [128, KC, T], F32,  offset=32 * KB)
    x1f1  = A("x1f1",  [128, KC, T], F32R, offset=32 * KB)
    x1b0  = A("x1b0",  [128, KC, T], BF16, offset=64 * KB)
    x1b1  = A("x1b1",  [128, KC, T], BF16, offset=80 * KB)
    sqbA  = A("sqbA",  [128, 8, T], BF16,  offset=80 * KB)
    lsb   = A("lsb",   [8, T], F32R,       offset=88 * KB)
    rowA  = A("rowA",  [1, T], F32R,       offset=90 * KB)
    rowB  = A("rowB",  [1, T], F32,        offset=92 * KB)
    rowC  = A("rowC",  [1, T], F32R,       offset=94 * KB)
    xfeat = A("xfeat", [128, KC, T], F32R, offset=96 * KB)
    actb0 = A("actb0", [128, KC, T], BF16, offset=96 * KB)
    actb1 = A("actb1", [128, KC, T], BF16, offset=112 * KB)
    sqbB  = A("sqbB",  [128, 8, T], BF16,  offset=96 * KB)
    rowA2 = A("rowA2", [1, T], F32R,       offset=104 * KB)
    rowB2 = A("rowB2", [1, T], F32,        offset=106 * KB)
    rowC2 = A("rowC2", [1, T], F32R,       offset=108 * KB)
    abuf  = A("abuf",  [128, KC, T], F32,  offset=128 * KB)
    gateT = A("gateT", [128, 3, KC, 256], BF16, offset=128 * KB)
    linbb = A("linbb", [128, 4, KC, 128], BF16, offset=152 * KB)
    xtb   = A("xtb",   [128, KC, T], F32,  offset=160 * KB)
    gelu0 = A("gelu0", [128, 4, T], BF16,  offset=168 * KB)
    T3    = A("T3",    [128, 2, T], F32,   offset=172 * KB)
    T4    = A("T4",    [128, 2, T], F32,   offset=176 * KB)
    cwb0  = A("cwb0",  [128, T], F32,      offset=180 * KB)
    cwb1  = A("cwb1",  [128, T], F32,      offset=182 * KB)
    rowE  = A("rowE",  [1, T], F32R,       offset=184 * KB)
    cwT1  = A("cwT1",  [8, T], F32R,       offset=186 * KB)
    PRM = 192 * KB
    igwin = A("igwin", [128, 4, 128], F32R, offset=PRM)
    agwin = A("agwin", [128, 4, 128], F32R, offset=PRM + 2 * KB)
    sm32  = A("sm32",  [128, S32], F32, offset=PRM + 4 * KB)
    smr   = A("smr",   [128, SR], F32R, offset=PRM + 4 * KB + 1472)
    cwT0  = A("cwT0",  [8, T], F32R,    offset=PRM + 6 * KB)
    nsp   = A("nsp",   [128, NH], F32,  offset=PRM + 8 * KB)
    spt   = A("spt",   [128, NH], F32,  offset=PRM + 8 * KB + 64)
    carry = A("carry", [128, KC], F32,  offset=PRM + 8 * KB + 128)
    ones_bf = A("ones_bf", [128, 1], BF16, offset=PRM + 8 * KB + 192)
    rlp   = A("rlp",   [128, KC, 8], F32R, offset=PRM + 8 * KB + 224)
    TIN = PRM + 8 * KB + 736
    cwsb  = A("cwsb",  [128, 8], F32,  offset=TIN)
    m1sb  = A("m1sb",  [128, 8], F32,  offset=TIN + 32)
    m2sb  = A("m2sb",  [128, 8], F32,  offset=TIN + 64)
    exsb  = A("exsb",  [128, 8], F32,  offset=TIN + 96)
    l2sb  = A("l2sb",  [128, 8], F32,  offset=TIN + 128)
    colA  = A("colA",  [128, 1], F32,  offset=TIN + 160)
    colB  = A("colB",  [128, 1], F32,  offset=TIN + 192)

    apar_c = sm32.ap()[:, OFF_APAR:OFF_APAR + 16]
    igb_c = sm32.ap()[:, OFF_IGB:OFF_IGB + 16]
    agb_c = sm32.ap()[:, OFF_AGB:OFF_AGB + 16]
    rs_c = sm32.ap()[:, OFF_RS:OFF_RS + 16]
    rl_c = sm32.ap()[:, OFF_RL:OFF_RL + 128]
    pes_c = sm32.ap()[:, OFF_PES:OFF_PES + 8]
    id128 = sm32.ap()[:, OFF_ID128:OFF_ID128 + 128]
    id8 = sm32.ap()[0:8, OFF_ID8:OFF_ID8 + 8]
    rsta_c = sm32.ap()[:, OFF_RSTA:OFF_RSTA + 8]
    rstb_c = sm32.ap()[:, OFF_RSTB:OFF_RSTB + 8]
    for i, val in enumerate(FCONSTS):
        nc.const_aps.register(F32, val, sm32.ap()[:, OFF_FC + i:OFF_FC + i + 1])
    onescol_r = smr.ap()[:, 0:1]
    onesrow_r = smr.ap()[0:1, 0:128]
    ones8row_r = smr.ap()[0:1, 0:8]

    es = ExitStack()
    P = [es.enter_context(nc.psum_tensor(f"P{i}", [128, T], F32)) for i in range(8)]
    SEMS = {}
    for s in ("SX", "SW", "SP", "SA", "SV", "SL", "SO"):
        SEMS[s] = es.enter_context(nc.semaphore(s))
    block = es.enter_context(nc.Block())

    DMA_SEMS = {"SX", "SW", "SO"}
    streams = {e: [] for e in ("sync", "tensor", "scalar", "vector", "gpsimd")}
    cnt = {s: 0 for s in SEMS}

    def ev(eng, fn, waits=None, inc=None):
        streams[eng].append((dict(waits or {}), fn, inc))
        if inc:
            cnt[inc] += 16 if inc in DMA_SEMS else 1
            return cnt[inc]
        return None

    def merge(*wds):
        out = {}
        for wd in wds:
            if not wd:
                continue
            for k, v in wd.items():
                out[k] = max(out.get(k, 0), v)
        return out

    w_sm = {}
    v = ev("sync", lambda e: e.dma_start(sm32.ap()[:, :], sm32_d), inc="SX")
    v = ev("sync", lambda e: e.dma_start(smr.ap()[:, :], smr_d), inc="SX")
    w_sm["SX"] = v

    ev("scalar", lambda e: e.activation(spt.ap()[:, :], apar_c, AF.Abs), waits=w_sm)
    ev("scalar", lambda e: e.activation(spt.ap()[:, :], spt.ap()[:, :], AF.Exp, scale=-1.0))
    v = ev("scalar", lambda e: e.activation(spt.ap()[:, :], spt.ap()[:, :], AF.Ln, bias=1.0), inc="SA")
    v = ev("vector", lambda e: e.tensor_scalar_max(nsp.ap()[:, :], apar_c, 0.0),
           waits=merge({"SA": v}, w_sm), inc="SV")
    v = ev("vector", lambda e: e.tensor_add(nsp.ap()[:, :], nsp.ap()[:, :], spt.ap()[:, :]),
           waits={"SV": v}, inc="SV")
    v = ev("vector", lambda e: e.tensor_scalar_mul(nsp.ap()[:, :], nsp.ap()[:, :], -8.0),
           waits={"SV": v}, inc="SV")
    for k in range(KC):
        def frl(e, k=k):
            return e.tensor_scalar_mul(rlp.ap()[:, k, :], rl_c[:, k * 8:(k + 1) * 8], rs_c[:, k:k + 1])
        v = ev("vector", frl, inc="SV")
    v = ev("vector", lambda e: e.memset(carry.ap()[:, :], 0.0), inc="SV")
    v = ev("vector", lambda e: e.memset(ones_bf.ap()[:, :], 1.0), inc="SV")
    w_setup_v = {"SV": cnt["SV"], "SA": cnt["SA"]}

    bankfree = [dict() for _ in range(8)]
    xfeat_free, abuf_free, xtb_free, ybuf_free = {}, {}, {}, {}
    gateT_free = [dict(), dict(), dict()]
    linb_free = [dict(), dict(), dict(), dict()]
    T3_free = [dict(), dict()]
    T4_free = [dict(), dict()]
    gelu_free = [dict() for _ in range(4)]
    actb_free = [dict(), dict()]
    igwin_free = [dict() for _ in range(4)]
    agwin_free = [dict() for _ in range(4)]
    state = {"carry_v": cnt["SV"], "gm1": 0, "wtile": 0, "glob_e": 0,
             "rowE_free": dict(), "last_igw_mm": 0, "lin_last": [0, 0],
             "scan_sv": [cnt["SV"]] * KC}

    def attn_chunk(cidx, half):
        w_x = {}
        v = ev("sync", lambda e, c=cidx: e.dma_start(xfeat.ap()[:, :, :], xin[c]),
               waits=merge(xfeat_free), inc="SX")
        w_x["SX"] = v
        ig_w = [None] * KC
        ag_w = [None] * KC
        def emit_win_load(hc):
            sl = state["wtile"] % 4
            state["wtile"] += 1
            def fli(e, hc=hc, sl=sl):
                return e.dma_start(igwin.ap()[:, sl, :], igw_d[hc])
            v = ev("gpsimd", fli, waits=igwin_free[sl], inc="SW")
            ig_w[hc] = ({"SW": v}, sl)
            def fla(e, hc=hc, sl=sl):
                return e.dma_start(agwin.ap()[:, sl, :], agw_d[hc])
            v = ev("gpsimd", fla, waits=agwin_free[sl], inc="SW")
            ag_w[hc] = ({"SW": v}, sl)
        for hc in range(4):
            emit_win_load(hc)
        ig_done = [0] * KC
        ag_done = [0] * KC
        sig_done = [0] * KC
        for hc in range(KC):
            pig, pag = P[2 * (hc % 2)], P[1 + 2 * (hc % 2)]
            wb = merge(w_x, bankfree[2 * (hc % 2)], bankfree[1 + 2 * (hc % 2)], w_setup_v)
            wi, sli = ig_w[hc]
            wa, sla = ag_w[hc]
            def fig(e, pig=pig, hc=hc, sli=sli):
                return e.matmul(pig.ap()[:, :], igwin.ap()[:, sli, :], xfeat.ap()[:, hc, :],
                                start=True, stop=True)
            ig_done[hc] = ev("tensor", fig, waits=merge(wb, wi), inc="SP")
            igwin_free[sli] = {"SP": ig_done[hc]}
            def fag(e, pag=pag, hc=hc, sla=sla):
                return e.matmul(pag.ap()[:, :], agwin.ap()[:, sla, :], xfeat.ap()[:, hc, :],
                                start=True, stop=True)
            ag_done[hc] = ev("tensor", fag, waits=wa, inc="SP")
            agwin_free[sla] = {"SP": ag_done[hc]}
            if hc + 4 < KC:
                emit_win_load(hc + 4)
            def fsx(e, pig=pig, hc=hc):
                return e.activation(xtb.ap()[:, hc, :], pig.ap()[:, :], AF.Sigmoid,
                                    bias=igb_c[:, hc:hc + 1])
            ev("scalar", fsx, waits=merge({"SP": ig_done[hc]},
                                          {"SV": state["scan_sv"][hc]}))
            def fsa(e, pag=pag, hc=hc):
                return e.activation(abuf.ap()[:, hc, :], pag.ap()[:, :], AF.Sigmoid,
                                    bias=agb_c[:, hc:hc + 1])
            sig_done[hc] = ev("scalar", fsa, waits={"SP": ag_done[hc]}, inc="SA")
            bankfree[2 * (hc % 2)] = {"SA": sig_done[hc]}
            bankfree[1 + 2 * (hc % 2)] = {"SA": sig_done[hc]}
        state["last_igw_mm"] = ag_done[KC - 1]
        a_exp = [0, 0]
        for w in range(2):
            for i in range(8):
                hc = w * 8 + i
                def fexp(e, hc=hc):
                    return e.activation(abuf.ap()[:, hc, :], abuf.ap()[:, hc, :], AF.Exp,
                                        scale=nsp.ap()[:, hc:hc + 1])
                a_exp[w] = ev("scalar", fexp, inc="SA")
        yf = ybuf.ap().rearrange("p k t -> p (k t)")
        xff = xfeat.ap().rearrange("p k t -> p (k t)")
        H = lambda t, w: t.ap()[:, 8 * w:8 * w + 8, :].rearrange("p k t -> p (k t)")
        v6 = [0, 0]
        v7 = [0, 0]
        for w in range(2):
            def f6(e, w=w):
                return e.tensor_mul(H(xtb, w), H(xtb, w), H(xfeat, w))
            v6[w] = ev("gpsimd", f6,
                       waits=merge({"SA": sig_done[8 * w + 7]}, {"SV": state["carry_v"]},
                                   w_x if w == 0 else None), inc="SL")
        xt1_last = v6[1]
        vq = [0, 0]
        for w in range(2):
            def f2(e, w=w):
                return e.tensor_mul(H(ybuf, w), H(abuf, w), H(abuf, w))
            v2 = ev("vector", f2, waits={"SA": a_exp[w]}, inc="SV")
            def f3(e, w=w):
                return e.activation(H(ybuf, w), H(ybuf, w), AF.Sqrt, scale=-1.0, bias=1.0)
            vq[w] = ev("scalar", f3, waits={"SV": v2}, inc="SA")
        for w in range(2):
            def frst(e, w=w, c=cidx):
                return e.tensor_scalar(ybuf.ap()[:, 8 * w:8 * w + 8, 0:1],
                                       ybuf.ap()[:, 8 * w:8 * w + 8, 0:1],
                                       rsta_c[:, c:c + 1], rstb_c[:, c:c + 1],
                                       op0=ALU.mult, op1=ALU.add)
            v5 = ev("vector", frst, waits={"SA": vq[w]}, inc="SV")
            def f7(e, w=w):
                return e.tensor_mul(H(xtb, w), H(xtb, w), H(ybuf, w))
            v7[w] = ev("vector", f7, waits={"SV": v5, "SL": v6[w]}, inc="SV")
            for i in range(8):
                hc = w * 8 + i
                def fsc(e, hc=hc):
                    return e.tensor_tensor_scan(ybuf.ap()[:, hc, :], abuf.ap()[:, hc, :],
                                                xtb.ap()[:, hc, :], carry.ap()[:, hc:hc + 1],
                                                op0=ALU.mult, op1=ALU.add)
                v = ev("vector", fsc, inc="SV")
                state["scan_sv"][hc] = v
        def fcu(e):
            return e.tensor_copy(carry.ap()[:, :], ybuf.ap()[:, :, T - 1:T])
        v = ev("vector", fcu, waits={"SV": v}, inc="SV")
        state["carry_v"] = v
        if half is not None:
            dst = x1f0 if half == 0 else x1f1
            def fr(e, dst=dst):
                return e.tensor_add(dst.ap().rearrange("p k t -> p (k t)"), xff, yf)
            v = ev("vector", fr, inc="SV")
            xfeat_free.clear()
            xfeat_free["SV"] = v
        else:
            xfeat_free.clear()
            xfeat_free["SL"] = xt1_last
            xfeat_free["SP"] = ag_done[KC - 1]
        abuf_free.clear()
        abuf_free["SV"] = v
        xtb_free.clear()
        xtb_free["SV"] = v
        ybuf_free.clear()
        ybuf_free["SV"] = v
        return v

    def layernorm(x1, eps, wpre, scr, bs0, bs1, bb0, bb1):
        sqb, rA, rB, rC = scr
        sq_done = [[0] * 8, [0] * 8]
        pe_wave = [0, 0]
        for wv in range(2):
            for i in range(8):
                hc = wv * 8 + i
                wq = wpre if (wv == 0 and i == 0) else None
                if wv == 1:
                    wq = merge(wq, {"SP": pe_wave[0]})
                def fq(e, hc=hc, i=i):
                    return e.activation(sqb.ap()[:, i, :], x1.ap()[:, hc, :], AF.Square)
                sq_done[wv][i] = ev("scalar", fq, waits=wq, inc="SA")
            for i in range(8):
                hc = wv * 8 + i
                w0 = merge(bankfree[bs1], {"SA": sq_done[wv][i]}, wpre) if (wv == 0 and i == 0) \
                    else {"SA": sq_done[wv][i]}
                def fs2(e, i=i, hc=hc):
                    return e.matmul(P[bs1].ap()[0:1, :], ones_bf.ap()[:, :], sqb.ap()[:, i, :],
                                    start=(hc == 0), stop=(hc == KC - 1))
                pe_wave[wv] = ev("tensor", fs2, waits=w0, inc="SP")
        sq_sum_done = pe_wave[1]
        for hc in range(KC):
            def fs(e, hc=hc):
                return e.matmul(P[bs0].ap()[0:1, :], onescol_r, x1.ap()[:, hc, :],
                                start=(hc == 0), stop=(hc == KC - 1))
            v = ev("tensor", fs, waits=merge(bankfree[bs0], wpre, w_sm) if hc == 0 else None,
                   inc="SP" if hc == KC - 1 else None)
        sum_done = v
        v = ev("vector", lambda e: e.tensor_scalar_mul(rA.ap()[:, :], P[bs0].ap()[0:1, :], 1.0 / D),
               waits=merge({"SP": sum_done}, wpre), inc="SV")
        v = ev("vector", lambda e: e.scalar_tensor_tensor(rB.ap()[:, :], rA.ap()[:, :], -1.0,
                                                          rA.ap()[:, :], op0=ALU.mult, op1=ALU.mult),
               waits=merge({"SV": v}, wpre), inc="SV")
        v = ev("vector", lambda e: e.scalar_tensor_tensor(rB.ap()[:, :], P[bs1].ap()[0:1, :], 1.0 / D,
                                                          rB.ap()[:, :], op0=ALU.mult, op1=ALU.add),
               waits={"SP": sq_sum_done, "SV": v}, inc="SV")
        bankfree[bs0] = {"SV": v}
        bankfree[bs1] = {"SV": v}
        ev("scalar", lambda e: e.activation(rB.ap()[:, :], rB.ap()[:, :], AF.Ln, bias=eps),
           waits={"SV": v})
        va = ev("scalar", lambda e: e.activation(rC.ap()[:, :], rB.ap()[:, :], AF.Exp, scale=-0.5),
                inc="SA")
        v2 = ev("vector", lambda e: e.scalar_tensor_tensor(rA.ap()[:, :], rA.ap()[:, :], -1.0,
                                                           rC.ap()[:, :], op0=ALU.mult, op1=ALU.mult),
                waits={"SA": va}, inc="SV")
        vb1 = ev("tensor", lambda e: e.matmul(P[bb0].ap()[:, :], onesrow_r, rC.ap()[:, :],
                                              start=True, stop=True),
                 waits=merge(bankfree[bb0], {"SA": va}), inc="SP")
        vb2 = ev("tensor", lambda e: e.matmul(P[bb1].ap()[:, :], onesrow_r, rA.ap()[:, :],
                                              start=True, stop=True),
                 waits=merge(bankfree[bb1], {"SV": v2}), inc="SP")
        for hc in range(KC):
            def fa1(e, hc=hc):
                return e.tensor_mul(x1.ap()[:, hc, :], x1.ap()[:, hc, :], P[bb0].ap()[:, :])
            ev("vector", fa1, waits={"SP": max(vb1, vb2)} if hc == 0 else None)
            def fa2(e, hc=hc):
                return e.tensor_add(x1.ap()[:, hc, :], x1.ap()[:, hc, :], P[bb1].ap()[:, :])
            v = ev("vector", fa2, inc="SV")
        bankfree[bb0] = {"SV": v}
        bankfree[bb1] = {"SV": v}
        return v

    def router(x1, cwT, ln_done, w_extra):
        sqr_done = [[0] * 8, [0] * 8]
        pe_wave = [0, 0]
        for wv in range(2):
            for i in range(8):
                hc = wv * 8 + i
                wq = {"SV": ln_done} if (wv == 0 and i == 0) else None
                if wv == 1:
                    wq = merge(wq, {"SP": pe_wave[0]})
                def fq(e, hc=hc, i=i):
                    return e.activation(sqbA.ap()[:, i, :], x1.ap()[:, hc, :], AF.Square)
                sqr_done[wv][i] = ev("scalar", fq, waits=wq, inc="SA")
            for i in range(8):
                hc = wv * 8 + i
                w0 = merge(bankfree[5], {"SA": sqr_done[wv][i]}) if (wv == 0 and i == 0) \
                    else {"SA": sqr_done[wv][i]}
                def fs2(e, i=i, hc=hc):
                    return e.matmul(P[5].ap()[0:1, :], ones_bf.ap()[:, :], sqbA.ap()[:, i, :],
                                    start=(hc == 0), stop=(hc == KC - 1))
                pe_wave[wv] = ev("tensor", fs2, waits=w0, inc="SP")
        v = ev("vector", lambda e: e.tensor_scalar_mul(rowB.ap()[:, :], P[5].ap()[0:1, :], 1.0 / D),
               waits={"SP": pe_wave[1]}, inc="SV")
        bankfree[5] = {"SV": v}
        ev("scalar", lambda e: e.activation(rowB.ap()[:, :], rowB.ap()[:, :], AF.Ln, bias=1e-6),
           waits={"SV": v})
        sc_done = ev("scalar", lambda e: e.activation(rowC.ap()[:, :], rowB.ap()[:, :], AF.Exp,
                                                      scale=-0.5, bias=-0.5 * float(np.log(D))),
                     inc="SA")
        for hc in range(KC):
            def fl(e, hc=hc):
                return e.matmul(P[4].ap()[0:8, :], rlp.ap()[:, hc, :], x1.ap()[:, hc, :],
                                start=(hc == 0), stop=(hc == KC - 1))
            v = ev("tensor", fl, waits=merge(bankfree[4], {"SV": ln_done}) if hc == 0 else None,
                   inc="SP" if hc == KC - 1 else None)
        logits_done = v
        scb_done = ev("tensor", lambda e: e.matmul(P[5].ap()[0:8, :], ones8row_r, rowC.ap()[:, :],
                                                   start=True, stop=True),
                      waits=merge(bankfree[5], {"SA": sc_done}), inc="SP")
        v = ev("vector", lambda e: e.tensor_copy(lsb.ap()[:, :], P[4].ap()[0:8, :]),
               waits={"SP": logits_done}, inc="SV")
        bankfree[4] = {"SV": v}
        v = ev("vector", lambda e: e.tensor_mul(lsb.ap()[:, :], lsb.ap()[:, :], P[5].ap()[0:8, :]),
               waits={"SP": scb_done}, inc="SV")
        bankfree[5] = {"SV": v}
        lsb_done = v
        for tc in range(4):
            def ftr(e, tc=tc):
                return e.transpose(P[6].ap()[:, 0:8], lsb.ap().bitcast(F32)[0:8, tc * 128:(tc + 1) * 128], id8)
            v = ev("tensor", ftr, waits=merge(bankfree[6], {"SV": lsb_done}, w_sm), inc="SP")
            v = ev("vector", lambda e: e.tensor_copy(cwsb.ap()[:, :], P[6].ap()[:, 0:8]),
                   waits={"SP": v}, inc="SV")
            bankfree[6] = {"SV": v}
            v = ev("vector", lambda e: e.reduce_max(colA.ap()[:, :], cwsb.ap()[:, :], axis=AX,
                                                    negate=True), waits={"SV": v}, inc="SV")
            v = ev("vector", lambda e: e.tensor_scalar(m1sb.ap()[:, :], cwsb.ap()[:, :],
                                                       colA.ap()[:, 0:1], 0.0,
                                                       op0=ALU.add, op1=ALU.is_ge),
                   waits={"SV": v}, inc="SV")
            vex = ev("scalar", lambda e: e.activation(exsb.ap()[:, :], cwsb.ap()[:, :], AF.Exp,
                                                      bias=colA.ap()[:, 0:1]),
                     waits={"SV": v}, inc="SA")
            v = ev("vector", lambda e: e.scalar_tensor_tensor(l2sb.ap()[:, :], m1sb.ap()[:, :], -1e30,
                                                              cwsb.ap()[:, :], op0=ALU.mult,
                                                              op1=ALU.add), waits={"SV": v}, inc="SV")
            v = ev("vector", lambda e: e.reduce_max(colB.ap()[:, :], l2sb.ap()[:, :], axis=AX,
                                                    negate=True), waits={"SV": v}, inc="SV")
            v = ev("vector", lambda e: e.tensor_scalar(m2sb.ap()[:, :], l2sb.ap()[:, :],
                                                       colB.ap()[:, 0:1], 0.0,
                                                       op0=ALU.add, op1=ALU.is_ge),
                   waits={"SV": v}, inc="SV")
            v = ev("vector", lambda e: e.tensor_add(m1sb.ap()[:, :], m1sb.ap()[:, :], m2sb.ap()[:, :]),
                   waits={"SV": v}, inc="SV")
            v = ev("vector", lambda e: e.tensor_mul(m1sb.ap()[:, :], m1sb.ap()[:, :], exsb.ap()[:, :]),
                   waits={"SA": vex, "SV": v}, inc="SV")
            v = ev("vector", lambda e: e.reduce_sum(colB.ap()[:, :], m1sb.ap()[:, :], axis=AX),
                   waits={"SV": v}, inc="SV")
            v = ev("vector", lambda e: e.reciprocal(colA.ap()[:, :], colB.ap()[:, :]),
                   waits={"SV": v}, inc="SV")
            v = ev("vector", lambda e: e.tensor_scalar_mul(cwsb.ap()[:, :], m1sb.ap()[:, :],
                                                           colA.ap()[:, 0:1]),
                   waits={"SV": v}, inc="SV")
            def ftr2(e, tc=tc):
                return e.transpose(P[7].ap()[0:8, 0:128], cwsb.ap()[:, 0:8], id128)
            v = ev("tensor", ftr2, waits=merge(bankfree[7], {"SV": v}), inc="SP")
            def fcp(e, tc=tc):
                return e.tensor_copy(cwT.ap()[:, tc * 128:(tc + 1) * 128], P[7].ap()[0:8, 0:128])
            v = ev("vector", fcp, waits=merge({"SP": v}, w_extra if tc == 0 else None), inc="SV")
            bankfree[7] = {"SV": v}
        return cnt["SV"]

    x1f_h = [x1f0, x1f1]
    x1b_h = [x1b0, x1b1]
    actb_h = [actb0, actb1]
    cwb_h = [cwb0, cwb1]
    cwT_h = [cwT0, cwT1]
    cw_done = [dict(), dict()]
    x1b_done = [dict(), dict()]
    act_last = [0, 0]
    x1f_last = [0, 0]

    def expert_pass(e_i, halves, w_first=None):
        cwb_ready = [None, None]
        for h in halves:
            def frow(e, h=h, e_i=e_i):
                return e.dma_start(rowE.ap()[0:1, :], cwT_h[h].ap()[e_i:e_i + 1, :])
            v = ev("sync", frow, waits=merge(cw_done[h], state["rowE_free"], w_first), inc="SX")
            def fcwb(e):
                return e.matmul(P[6].ap()[:, :], onesrow_r, rowE.ap()[0:1, :], start=True, stop=True)
            v = ev("tensor", fcwb, waits=merge(bankfree[6], {"SX": v}), inc="SP")
            state["rowE_free"] = {"SP": v}
            def fcwb2(e, h=h, e_i=e_i):
                return e.tensor_scalar(cwb_h[h].ap()[:, :], P[6].ap()[:, :], pes_c[:, e_i:e_i + 1],
                                       0.5, op0=ALU.mult, op1=ALU.mult)
            v = ev("vector", fcwb2, waits=merge({"SP": v}, w_sm, w_first), inc="SV")
            bankfree[6] = {"SV": v}
            cwb_ready[h] = {"SV": v}

        steps = [(h8, g) for h8 in range(8) for g in range(2)]
        gate_ready = {}
        gbase = state["glob_e"]

        def emit_gate_load(gs, guard=None):
            h8, g = steps[gs]
            buf = (gbase + gs) % 3
            def fgw(e, g=g, h8=h8, buf=buf, e_i=e_i):
                return e.dma_start(gateT.ap()[:, buf, :, :], gtw[e_i, g, h8])
            v = ev("gpsimd", fgw, waits=merge(gateT_free[buf], guard), inc="SW")
            gate_ready[gs] = ({"SW": v}, buf)
        for gs in range(3):
            emit_gate_load(gs, w_first)
        lin_ready = {}
        def emit_lin_load(m, guard=None):
            buf = m % 4
            def fln(e, m=m, buf=buf, e_i=e_i):
                return e.dma_start(linbb.ap()[:, buf, :, :], ltw[e_i, m])
            v = ev("gpsimd", fln, waits=merge(linb_free[buf], guard), inc="SW")
            lin_ready[m] = ({"SW": v}, buf)
        for m in range(4):
            emit_lin_load(m, w_first)
        first_evac = dict(w_first or {})
        first_gp = merge(w_first, xfeat_free)
        for gs, (h8, g) in enumerate(steps):
            gw_ready, buf = gate_ready.pop(gs)
            for m in range(2):
                for h in halves:
                    pi = state["gm1"] % 4
                    state["gm1"] += 1
                    pb = P[pi]
                    wb = merge(bankfree[pi], gw_ready, x1b_done[h])
                    for k in range(KC):
                        def fmm(e, buf=buf, m=m, k=k, pb=pb, h=h):
                            return e.matmul(pb.ap()[:, :],
                                            gateT.ap()[:, buf, k, m * 128:(m + 1) * 128],
                                            x1b_h[h].ap()[:, k, :],
                                            start=(k == 0), stop=(k == KC - 1))
                        v = ev("tensor", fmm, waits=wb if k == 0 else None,
                               inc="SP" if k == KC - 1 else None)
                    mm_done = v
                    tpar = (2 * m + h) % 2
                    t3 = T3.ap()[:, tpar, :]
                    t4 = T4.ap()[:, tpar, :]
                    slot = 2 * m + (h if len(halves) > 1 else halves[0] % 2)
                    if g == 0:
                        def f1(e, pb=pb, t3=t3):
                            return e.activation(t3, pb.ap()[:, :], AF.Square)
                        ev("scalar", f1, waits=merge({"SP": mm_done}, T3_free[tpar], first_evac))
                        def f2(e, t3=t3):
                            return e.activation(t3, t3, AF.Copy, scale=C3, bias=1.0)
                        va = ev("scalar", f2, inc="SA")
                        def f3(e, pb=pb, t3=t3, t4=t4):
                            return e.scalar_tensor_tensor(t4, pb.ap()[:, :], GC, t3,
                                                          op0=ALU.mult, op1=ALU.mult)
                        vv = ev("vector", f3, waits=merge({"SA": va}, T4_free[tpar], first_evac),
                                inc="SV")
                        T3_free[tpar] = {"SV": vv}
                        def f4(e, t4=t4):
                            return e.activation(t4, t4, AF.Tanh)
                        va = ev("scalar", f4, waits={"SV": vv}, inc="SA")
                        def f5(e, pb=pb, t4=t4, slot=slot):
                            return e.scalar_tensor_tensor(gelu0.ap()[:, slot, :], t4, 1.0,
                                                          pb.ap()[:, :], op0=ALU.add, op1=ALU.mult)
                        vv = ev("vector", f5, waits=merge({"SA": va}, gelu_free[slot], first_evac),
                                inc="SV")
                        T4_free[tpar] = {"SV": vv}
                        bankfree[pi] = {"SV": vv}
                    else:
                        def f6(e, pb=pb, t3=t3, h=h):
                            return e.tensor_mul(t3, pb.ap()[:, :], cwb_h[h].ap()[:, :])
                        vv = ev("vector", f6, waits=merge({"SP": mm_done}, cwb_ready[h],
                                                          T3_free[tpar], first_evac), inc="SV")
                        bankfree[pi] = {"SV": vv}
                        def f7(e, t3=t3, h8=h8, m=m, h=h, slot=slot):
                            return e.tensor_tensor(actb_h[h].ap()[:, h8 * 2 + m, :], t3,
                                                   gelu0.ap()[:, slot, :], op=ALU.mult)
                        vp = ev("gpsimd", f7, waits=merge({"SV": vv}, actb_free[h], first_gp),
                                inc="SL")
                        first_gp = None
                        actb_free[h] = {}
                        T3_free[tpar] = {"SL": vp}
                        gelu_free[slot] = {"SL": vp}
                        act_last[h] = vp
            first_evac = None
            gateT_free[buf] = {"SP": mm_done}
            if gs + 3 < len(steps):
                emit_gate_load(gs + 3)
        state["glob_e"] = (state["glob_e"] + len(steps)) % 3

        lr = 0
        lin_last_mm = [0, 0]
        for m in range(KC):
            lw, buf = lin_ready.pop(m)
            for h in halves:
                pb = P[4 + lr % 2]
                wb = merge(bankfree[4 + lr % 2], lw, {"SL": act_last[h]})
                for k in range(KC):
                    def fm2(e, m=m, k=k, pb=pb, buf=buf, h=h):
                        return e.matmul(pb.ap()[:, :], linbb.ap()[:, buf, k, :],
                                        actb_h[h].ap()[:, k, :], start=(k == 0), stop=(k == KC - 1))
                    v = ev("tensor", fm2, waits=wb if k == 0 else None,
                           inc="SP" if k == KC - 1 else None)
                mm2_done = v
                lin_last_mm[h] = v
                def fadd(e, m=m, pb=pb, h=h):
                    return e.tensor_add(x1f_h[h].ap()[:, m, :], x1f_h[h].ap()[:, m, :], pb.ap()[:, :])
                v = ev("vector", fadd, waits={"SP": mm2_done}, inc="SV")
                bankfree[4 + lr % 2] = {"SV": v}
                x1f_last[h] = v
                lr += 1
            linb_free[buf] = {"SP": mm2_done}
            if m + 4 < KC:
                emit_lin_load(m + 4)
        for h in halves:
            actb_free[h] = {"SP": lin_last_mm[h]}
            state["lin_last"][h] = lin_last_mm[h]

    for c in range(6):
        attn_chunk(c, None)
    r6 = attn_chunk(6, 0)
    r7 = attn_chunk(7, 1)
    ln1_h0 = layernorm(x1f0, 1e-5, {"SV": r6}, (sqbA, rowA, rowB, rowC), 4, 5, 6, 7)
    v = ev("scalar", lambda e: e.activation(
        x1b0.ap().rearrange("p k t -> p (k t)"), x1f0.ap().rearrange("p k t -> p (k t)"), AF.Copy),
        waits={"SV": ln1_h0}, inc="SA")
    x1b_done[0] = {"SA": v}
    router(x1f0, cwT0, ln1_h0, None)
    cw_done[0] = {"SV": cnt["SV"]}

    ln1_h1 = layernorm(x1f1, 1e-5, {"SV": r7, "SP": cnt["SP"]}, (sqbA, rowA, rowB, rowC), 4, 5, 6, 7)
    router(x1f1, cwT1, ln1_h1, merge(xtb_free))
    cw_done[1] = {"SV": cnt["SV"]}
    v = ev("scalar", lambda e: e.activation(
        x1b1.ap().rearrange("p k t -> p (k t)"), x1f1.ap().rearrange("p k t -> p (k t)"), AF.Copy),
        waits={"SV": cnt["SV"], "SP": cnt["SP"]}, inc="SA")
    x1b_done[1] = {"SA": v}

    w_first = merge(abuf_free, xtb_free, xfeat_free, {"SP": state["last_igw_mm"]})
    for e_i in range(NE):
        expert_pass(e_i, [0, 1], w_first=w_first if e_i == 0 else None)

    ln2_h0 = layernorm(x1f0, 1e-5,
                       merge({"SV": x1f_last[0]}, {"SP": state["lin_last"][0]}),
                       (sqbB, rowA2, rowB2, rowC2), 6, 7, 6, 7)
    vo = ev("sync", lambda e: e.dma_start(yout[0], x1f0.ap().bitcast(F32)[:, :, :]),
            waits={"SV": ln2_h0}, inc="SO")
    ln2_h1 = layernorm(x1f1, 1e-5,
                       merge({"SV": x1f_last[1]}, {"SP": cnt["SP"]}),
                       (sqbB, rowA2, rowB2, rowC2), 6, 7, 6, 7)
    vo = ev("sync", lambda e: e.dma_start(yout[1], x1f1.ap().bitcast(F32)[:, :, :]),
            waits={"SV": ln2_h1}, inc="SO")
    ev("sync", lambda e: e.nop(), waits={"SO": cnt["SO"], "SW": cnt["SW"], "SP": cnt["SP"],
                                         "SA": cnt["SA"], "SV": cnt["SV"], "SL": cnt["SL"],
                                         "SX": cnt["SX"]})

    def mk(name):
        def run(eng):
            done = {}
            for w, fn, inc in streams[name]:
                for s, val in w.items():
                    if done.get(s, -1) < val:
                        eng.wait_ge(SEMS[s], val)
                        done[s] = val
                ins = fn(eng)
                if inc:
                    ins.then_inc(SEMS[inc], 16 if inc in DMA_SEMS else 1)
        return run

    block.sync(mk("sync"))
    block.tensor(mk("tensor"))
    block.scalar(mk("scalar"))
    block.vector(mk("vector"))
    block.gpsimd(mk("gpsimd"))
    es.close()
    return nc



def _prep_inputs(ins):
    x = np.ascontiguousarray(ins["L0_attn_x"], dtype=np.float32)
    x_pre = np.ascontiguousarray(x.reshape(2, 8, T, KC, 128).transpose(0, 1, 4, 3, 2))
    ge = np.ascontiguousarray(ins["L0_ffn_gating_einsum"], dtype=np.float32)
    gtw = np.ascontiguousarray(
        ge.transpose(0, 1, 3, 2).reshape(NE, 2, KC, 128, 8, 256).transpose(0, 1, 4, 3, 2, 5))
    lin = np.ascontiguousarray(
        np.asarray(ins["L0_ffn_linear"], np.float32).reshape(NE, KC, 128, KC, 128)
        .transpose(0, 3, 2, 1, 4))
    igw = np.ascontiguousarray(ins["L0_attn_ig_w"], dtype=np.float32)
    agw = np.ascontiguousarray(ins["L0_attn_ag_w"], dtype=np.float32)

    sm32 = np.zeros((128, S32), np.float32)
    sm32[:, OFF_APAR:OFF_APAR + 16] = np.asarray(ins["L0_attn_a_param"], np.float32).reshape(16, 128).T
    sm32[:, OFF_IGB:OFF_IGB + 16] = np.asarray(ins["L0_attn_ig_b"], np.float32).T
    sm32[:, OFF_AGB:OFF_AGB + 16] = np.asarray(ins["L0_attn_ag_b"], np.float32).T
    sm32[:, OFF_RS:OFF_RS + 16] = np.asarray(ins["L0_ffn_router_scale"], np.float32).reshape(16, 128).T
    sm32[:, OFF_RL:OFF_RL + 128] = np.asarray(
        ins["L0_ffn_router_logits"], np.float32).reshape(16, 128, 8).transpose(1, 0, 2).reshape(128, 128)
    sm32[:, OFF_PES:OFF_PES + 8] = np.tile(
        np.asarray(ins["L0_ffn_per_expert_scale"], np.float32).reshape(1, 8), (128, 1))
    sm32[:, OFF_ID128:OFF_ID128 + 128] = np.eye(128, dtype=np.float32)
    sm32[0:8, OFF_ID8:OFF_ID8 + 8] = np.eye(8, dtype=np.float32)
    for i, val in enumerate(FCONSTS):
        sm32[:, OFF_FC + i] = np.float32(val)
    smr = np.ones((128, SR), np.float32)

    in_maps = []
    for j in range(NCORE):
        b, jm = j // 4, j % 4
        xc = np.zeros((NCHUNK, 128, KC, T), np.float32)
        nreal = 2 * jm + 2
        xc[NCHUNK - nreal:] = x_pre[b, :nreal]
        smc = sm32.copy()
        rst = np.zeros(8, np.float32)
        rst[NCHUNK - nreal] = 1.0
        smc[:, OFF_RSTA:OFF_RSTA + 8] = (1.0 - rst)[None, :]
        smc[:, OFF_RSTB:OFF_RSTB + 8] = rst[None, :]
        in_maps.append(dict(xin=xc, gtw=gtw, ltw=lin, igw_d=igw, agw_d=agw,
                            sm32_d=smc, smr_d=smr))
    return in_maps


def _unpack_out(results):
    out = np.empty((2, 4096, D), np.float32)
    for j in range(NCORE):
        b, jm = j // 4, j % 4
        arr = results[j]["yout"]
        for h in range(2):
            out[b, jm * 1024 + h * T: jm * 1024 + (h + 1) * T, :] = \
                arr[h].transpose(2, 1, 0).reshape(T, D)
    return out


_NC_CACHE = {}


def kernel(**inputs):
    if "prog" not in _NC_CACHE:
        _NC_CACHE["prog"] = build_program()
    nc = _NC_CACHE["prog"]
    in_maps = _prep_inputs(inputs)
    res = run_bass_kernel_spmd(nc, in_maps, core_ids=list(range(NCORE)))
    return _unpack_out(res.results)
